# version 05
# baseline (speedup 1.0000x reference)
import numpy as np
import concourse.bass as bass
import concourse.mybir as mybir
from concourse.bass_utils import run_bass_kernel_spmd

F32 = mybir.dt.float32
BF16 = mybir.dt.bfloat16
AF = mybir.ActivationFunctionType
ALU = mybir.AluOpType
AX = mybir.AxisListType

B, T, C = 16, 2048, 1024
NCORES = 8
SPC = B // NCORES
NH, HS, BDIM, DELTA = 16, 64, 16, 32
HID = 4096
EPS = 1e-5
NBLK = T // 128
TW2 = 256
ENGINES = ("tensor", "vector", "scalar", "gpsimd", "sync")
VTS = 17 * 1040


class Prog:

    def __init__(self):
        self.ops = {e: [] for e in ENGINES}
        self.v = {}
        self.dma_sems = set()
        self.ckpts = []

    def ckpt(self, name):
        self.ckpts.append((name, {e: len(self.ops[e]) for e in ENGINES},
                           dict(self.v)))

    def cut(self, name, sems):
        for nm, lens, vals in self.ckpts:
            if nm == name:
                for e in ENGINES:
                    self.ops[e] = self.ops[e][:lens[e]]
                snap = [(s, v) for s, v in vals.items() if v > 0]
                for e in ENGINES:
                    self.ops[e].append((snap, None, None, 0))
                return
        raise KeyError(name)

    def val(self, sem):
        return self.v.get(sem, 0)

    def op(self, eng, fn, waits=(), inc=None, n=1):
        waits = [(sem, self.val(sem) if sem in self.dma_sems else v)
                 for (sem, v) in waits]
        self.ops[eng].append((list(waits), fn, inc, n))
        if inc is not None:
            self.v[inc] = self.v.get(inc, 0) + n
            return self.v[inc]
        return None

    def wait(self, eng, waits):
        self.ops[eng].append((list(waits), None, None, 0))

    def barrier(self, sems):
        snap = [(s, self.val(s)) for s in sems if self.val(s) > 0]
        for e in ENGINES:
            self.ops[e].append((snap, None, None, 0))

    def replay(self, eng_obj, name):
        autodrain = name in ("vector", "scalar", "gpsimd")
        for waits, fn, inc, n in self.ops[name]:
            for sem, val in waits:
                if val > 0:
                    eng_obj.wait_ge(sem, val)
            if fn is not None:
                if autodrain:
                    eng_obj.drain()
                inst = fn(eng_obj)
                if inc is not None:
                    inst.then_inc(inc, n)


def build_program(cut=None):
    nc = bass.Bass("TRN2", target_bir_lowering=False)
    p = Prog()

    dx = nc.dram_tensor("x", [SPC, T, C], F32, kind="ExternalInput")
    dout = nc.dram_tensor("out", [SPC, T, C], F32, kind="ExternalOutput")
    d_wdisp = nc.dram_tensor("wdisp", [C, 256], F32, kind="ExternalInput")
    d_wval = nc.dram_tensor("wval", [C, C], F32, kind="ExternalInput")
    d_wcproj = nc.dram_tensor("wcproj", [C, C], F32, kind="ExternalInput")
    d_w1t = nc.dram_tensor("w1t", [C, HID], F32, kind="ExternalInput")
    d_w2t = nc.dram_tensor("w2t", [HID, C], F32, kind="ExternalInput")
    d_sdw = nc.dram_tensor("sdw", [2, 128, 128], F32, kind="ExternalInput")
    d_sdot = nc.dram_tensor("sdot", [16, 128, 128], F32, kind="ExternalInput")
    d_ident = nc.dram_tensor("ident", [128, 128], F32, kind="ExternalInput")
    d_cvec = nc.dram_tensor("cvec", [128, 8], F32, kind="ExternalInput")
    d_wposx = nc.dram_tensor("wposx", [16, 128], F32, kind="ExternalInput")
    d_relt = nc.dram_tensor("relt", [16, 32], F32, kind="ExternalInput")
    d_ln1g = nc.dram_tensor("ln1g", [128, 8], F32, kind="ExternalInput")
    d_ln1b = nc.dram_tensor("ln1b", [128, 8], F32, kind="ExternalInput")
    d_b1 = nc.dram_tensor("b1c", [128, 32], F32, kind="ExternalInput")
    d_g2 = nc.dram_tensor("g2c", [128, 32], F32, kind="ExternalInput")
    d_b2ln = nc.dram_tensor("b2lnc", [128, 32], F32, kind="ExternalInput")
    d_b2f = nc.dram_tensor("b2fc", [128, 8], F32, kind="ExternalInput")
    d_ones = nc.dram_tensor("onesrow", [1, 128], F32, kind="ExternalInput")
    d_mask0 = nc.dram_tensor("mask0", [128, 1024], F32, kind="ExternalInput")
    d_skm = nc.dram_tensor("skm", [128, 32], F32, kind="ExternalInput")
    d_cp = nc.dram_tensor("cpscratch", [SPC, 8, 128, T], BF16)

    from contextlib import ExitStack
    with ExitStack() as _st:
        def SB(name, shape, dt):
            return _st.enter_context(nc.sbuf_tensor(name, shape, dt))
        identb = SB("identb", [128, 128], BF16)
        identf = SB("identf", [128, 128], F32)
        cbias = SB("cbias", [128, 32], F32)
        cvec = SB("s_cvec", [128, 8], F32)
        onescb = SB("onescb", [128, 1], BF16)
        onescf = SB("onescf", [128, 1], F32)
        onesrow = SB("s_onesrow", [1, 128], F32)
        ln1gb = SB("ln1gb", [128, 8], F32)
        ln1bb = SB("ln1bb", [128, 8], F32)
        b1c = SB("s_b1c", [128, 32], F32)
        g2c = SB("s_g2c", [128, 32], F32)
        b2lnc = SB("s_b2lnc", [128, 32], F32)
        b2fc = SB("s_b2fc", [128, 8], F32)
        stat = SB("stat", [128, 16], F32)
        constt = SB("constt", [128, 4], F32)
        nc.const_aps.register(F32, 0.0, constt[:, 0:1])
        nc.const_aps.register(F32, 1.0, constt[:, 1:2])
        nc.const_aps.register(F32, float(EPS), constt[:, 2:3])
        sD = _st.enter_context(nc.semaphore("sD"))
        sG = _st.enter_context(nc.semaphore("sG"))
        sT = _st.enter_context(nc.semaphore("sT"))
        sV = _st.enter_context(nc.semaphore("sV"))
        sS = _st.enter_context(nc.semaphore("sS"))
        sP = _st.enter_context(nc.semaphore("sP"))
        sO = _st.enter_context(nc.semaphore("sO"))
        sX = _st.enter_context(nc.semaphore("sX"))
        True
        SEMS = (sD, sG, sT, sV, sS, sP, sO, sX)
        p.dma_sems = {sD, sG, sO, sX}
        L = dict(locals())

        for dst, src in [(identf, d_ident), (cvec, d_cvec), (ln1gb, d_ln1g),
                         (ln1bb, d_ln1b), (b1c, d_b1), (g2c, d_g2),
                         (b2lnc, d_b2ln), (b2fc, d_b2f), (onesrow, d_ones)]:
            p.op("sync", lambda s, dst=dst, src=src: s.dma_start(dst[:, :], src[:, :]),
                 inc=sD, n=16)
        p.op("gpsimd", lambda g: g.dma_start(identb[:, :], d_ident[:, :]), inc=sG, n=16)
        consts_D = p.val(sD)
        p.ckpt("c1")
        p.op("vector", lambda v: v.memset(constt[:, 0:1], 0.0), inc=sV)
        p.op("vector", lambda v: v.memset(constt[:, 1:2], 1.0), inc=sV)
        p.op("vector", lambda v: v.memset(constt[:, 2:3], float(EPS)), inc=sV)
        p.op("vector", lambda v: v.memset(onescb[:, :], 1.0), inc=sV)
        p.op("vector", lambda v: v.memset(onescf[:, :], 1.0), inc=sV)
        p.ckpt("c2")

        with nc.sbuf_tensor("s_wposx", [16, 128], F32) as wposx, \
             nc.sbuf_tensor("s_relt", [16, 32], F32) as relt, \
             nc.psum_tensor("pc0", [128, 32], F32) as pc0:
            p.op("sync", lambda s: s.dma_start(wposx[:, :], d_wposx[:, :]), inc=sD, n=16)
            p.op("sync", lambda s: s.dma_start(relt[:, :], d_relt[:, :]), inc=sD, n=16)
            pw_D = p.val(sD)
            p.ckpt("c3")
            t_pf = p.op("tensor",
                        lambda t: t.matmul(pc0[:, :], wposx[:, :], relt[:, :],
                                           start=True, stop=True),
                        waits=[(sD, pw_D)], inc=sT)
            p.op("vector",
                 lambda v: v.tensor_scalar(cbias[:, :], pc0[:, :], cvec[:, 0:1],
                                           None, ALU.add),
                 waits=[(sT, t_pf), (sD, consts_D)], inc=sV)
        L["cbias_V"] = p.val(sV)
        p.barrier(SEMS)
        p.ckpt("ph0")

        with ExitStack() as _s1:
            def SB1(name, shape, dt):
                return _s1.enter_context(nc.sbuf_tensor(name, shape, dt))
            wdisp = SB1("s_wdisp", [128, 8 * 256], BF16)
            wval = SB1("s_wval", [128, 8 * 1024], BF16)
            wcproj = SB1("s_wcproj", [128, 8 * 1024], BF16)
            sdw = SB1("s_sdw", [128, 2 * 128], BF16)
            sdot = SB1("s_sdot", [128, 16 * 128], BF16)
            xntok = SB1("xntok", [128, 1024], BF16)
            xnfm = SB1("xnfm", [128, 8 * 2048], BF16)
            dwseq = SB1("dwseq", [128, 4 * 2080], BF16)
            dispfm = SB1("dispfm", [128, 2 * 512], BF16)
            valfm = SB1("valfm", [128, 8 * 512], BF16)
            valtok = SB1("valtok", [128, VTS], BF16)
            attntok = SB1("attntok", [128, 4 * 1024], BF16)
            ebuf = SB1("ebuf", [128, 2 * 512], BF16)
            gbuf = SB1("gbuf", [128, 2 * 512], BF16)
            sigb = SB1("sigb", [128, 2 * 512], F32)
            bondb = SB1("bondb", [128, 2 * 512], F32)
            stok = SB1("stok", [128, 64], F32)
            stok2 = SB1("stok2", [128, 64], F32)
            expw = SB1("expw", [128, 4 * 1024], BF16)
            rot = SB1("rot", [128, 512], BF16)
            tdd = SB1("tdd", [128, 512], BF16)
            tee = SB1("tee", [128, 512], BF16)
            dd = SB1("dd", [128, 512], BF16)
            ee = SB1("ee", [128, 512], BF16)
            rz = SB1("rz", [128, 16], F32)
            sumA = SB1("sumA", [128, 65], F32)
            sumB = SB1("sumB", [128, 65], F32)
            attnfm = SB1("attnfm", [128, 8 * 512], BF16)
            cptile = SB1("cptile", [128, 512], BF16)
            mask0 = SB1("s_mask0", [128, 1024], BF16)
            skm = SB1("s_skm", [128, 32], F32)
            xtok = SB1("xtok", [128, 1024], F32)
            xnf32 = SB1("xnf32", [128, 1024], F32)
            L["xtok"] = xtok
            L["xnf32"] = xnf32
            True
            L.update(dict(wdisp=wdisp, wval=wval, wcproj=wcproj, sdw=sdw,
                          sdot=sdot, xntok=xntok, xnfm=xnfm, dwseq=dwseq,
                          dispfm=dispfm, valfm=valfm, valtok=valtok,
                          attntok=attntok, ebuf=ebuf, gbuf=gbuf, sigb=sigb,
                          bondb=bondb, stok=stok, stok2=stok2, expw=expw, rot=rot, tdd=tdd,
                          tee=tee, dd=dd, ee=ee, rz=rz, sumA=sumA, sumB=sumB, attnfm=attnfm,
                          cptile=cptile))
            p.op("gpsimd", lambda g: g.dma_start(
                bass.AP(wdisp, 0, [[8 * 256, 128], [256, 8], [1, 256]]),
                bass.AP(d_wdisp, 0, [[256, 128], [128 * 256, 8], [1, 256]])), inc=sG, n=16)
            p.op("gpsimd", lambda g: g.dma_start(
                bass.AP(wval, 0, [[8 * 1024, 128], [1024, 8], [1, 1024]]),
                bass.AP(d_wval, 0, [[1024, 128], [128 * 1024, 8], [1, 1024]])), inc=sG, n=16)
            p.op("gpsimd", lambda g: g.dma_start(
                bass.AP(wcproj, 0, [[8 * 1024, 128], [1024, 8], [1, 1024]]),
                bass.AP(d_wcproj, 0, [[1024, 128], [128 * 1024, 8], [1, 1024]])), inc=sG, n=16)
            p.op("gpsimd", lambda g: g.dma_start(
                bass.AP(sdw, 0, [[2 * 128, 128], [128, 2], [1, 128]]),
                bass.AP(d_sdw, 0, [[128, 128], [128 * 128, 2], [1, 128]])), inc=sG, n=16)
            p.op("gpsimd", lambda g: g.dma_start(
                bass.AP(sdot, 0, [[16 * 128, 128], [128, 16], [1, 128]]),
                bass.AP(d_sdot, 0, [[128, 128], [128 * 128, 16], [1, 128]])), inc=sG, n=16)
            p.op("gpsimd", lambda g: g.dma_start(mask0[:, :], d_mask0[:, :]), inc=sG, n=16)
            p.op("sync", lambda s_: s_.dma_start(skm[:, :], d_skm[:, :]), inc=sD, n=16)
            L["wts_G"] = p.val(sG)
            L["mask0"] = mask0
            L["skm"] = skm

            for seq in range(SPC):
                phase1_seq(nc, p, seq, L)
                p.ckpt(f"p1seq{seq}")

        p.barrier(SEMS)

        with ExitStack() as _s2:
            def SB2(name, shape, dt):
                return _s2.enter_context(nc.sbuf_tensor(name, shape, dt))
            w1q = SB2("w1q", [128, 8 * 4096], BF16)
            w2q = SB2("w2q", [128, 32 * 1024], BF16)
            scol = SB2("scol", [128, 8], F32)
            qrow = SB2("qrow", [1, 8], F32)
            ucol = SB2("ucol", [128, 8], F32)
            vcol = SB2("vcol", [128, 8], F32)
            ubias = SB2("ubias", [128, 8], F32)
            b2lnb = SB2("b2lnb", [128, 32], BF16)
            True
            L.update(dict(w1q=w1q, w2q=w2q, scol=scol, qrow=qrow, ucol=ucol,
                          vcol=vcol, ubias=ubias, b2lnb=b2lnb))
            quantize_weights(nc, p, L)
            p.barrier(SEMS)
            p.ckpt("quant")

            with ExitStack() as _s3:
                def SB3(name, shape, dt):
                    return _s3.enter_context(nc.sbuf_tensor(name, shape, dt))
                xtok2 = SB3("xtok2", [128, 3 * 2048], F32)
                xn2 = SB3("xn2", [128, 1024], F32)
                ln1tok = SB3("ln1tok", [128, 1024], BF16)
                ln1fm = SB3("ln1fm", [128, 2 * 8 * 256], BF16)
                h1g = SB3("h1g", [128, 32 * 256], BF16)
                sqt = SB3("sqt", [128, 256], BF16)
                rrow = SB3("rrow", [1, 4 * 256], F32)
                rrep = SB3("rrep", [128, 256], F32)
                murep = SB3("murep", [128, 256], F32)
                cptw = SB3("cptw", [128, 2 * 8 * 256], BF16)
                sumt = SB3("sumt", [128, 256], F32)
                outtok = SB3("outtok", [128, 2 * 1024], F32)
                psc = _s3.enter_context(nc.psum_tensor("psc", [128, 512], F32))
                True
                L.update(dict(xtok2=xtok2, xn2=xn2, ln1tok=ln1tok,
                              ln1fm=ln1fm, h1g=h1g, sqt=sqt, rrow=rrow,
                              rrep=rrep, murep=murep, cptw=cptw,
                              sumt=sumt, outtok=outtok, psc=psc))
                ntiles = SPC * T // TW2
                phase2_front(nc, p, 0, L)
                phase2_front(nc, p, 1, L)
                for tile in range(ntiles):
                    phase2_back(nc, p, tile, L)
                    if tile == 0:
                        p.ckpt("tile0")
                    if tile + 2 < ntiles:
                        phase2_front(nc, p, tile + 2, L)

        if cut is not None:
            p.cut(cut, SEMS)
        else:
            p.op("sync", lambda s: s.nop(),
                 waits=[(sD, p.val(sD)), (sG, p.val(sG)), (sO, p.val(sO)),
                        (sX, p.val(sX))])

        with nc.Block() as block:
            @block.tensor
            def _(e):
                p.replay(e, "tensor")

            @block.vector
            def _(e):
                p.replay(e, "vector")

            @block.scalar
            def _(e):
                p.replay(e, "scalar")

            @block.gpsimd
            def _(e):
                p.replay(e, "gpsimd")

            @block.sync
            def _(e):
                p.replay(e, "sync")

    return nc


def ln_stats_apply(nc, p, L, src, srcslice, out, outslice, waitin, scratch=None):
    sV, sS = L["sV"], L["sS"]
    stat = L["stat"]
    s0 = srcslice[0]
    m_v1 = p.op("vector", lambda v: v.tensor_reduce(
        stat[:, 1:2], src[:, s0:s0 + 1024], AX.X, ALU.add),
        waits=waitin, inc=sV)
    psc = scratch if scratch is not None else L["psc"]
    p.op("scalar", lambda sc: sc.activation(
        psc[:, 0:512], src[:, s0:s0 + 512], AF.Square,
        accum_out=stat[:, 11:12]), waits=waitin, inc=sS)
    m_s1 = p.op("scalar", lambda sc: sc.activation(
        psc[:, 0:512], src[:, s0 + 512:s0 + 1024], AF.Square,
        accum_out=stat[:, 12:13]), inc=sS)

    def f_v2(v):
        v.tensor_tensor(stat[:, 2:3], stat[:, 11:12], stat[:, 12:13], ALU.add)
        v.drain()
        v.tensor_scalar(stat[:, 3:4], stat[:, 1:2], 1.0 / C, None, ALU.mult)
        v.tensor_scalar(stat[:, 4:5], stat[:, 2:3], 1.0 / C, None, ALU.mult)
        v.drain()
        v.tensor_tensor(stat[:, 5:6], stat[:, 3:4], stat[:, 3:4], ALU.mult)
        v.drain()
        return v.tensor_tensor(stat[:, 6:7], stat[:, 4:5], stat[:, 5:6], ALU.subtract)
    m_v2 = p.op("vector", f_v2, waits=[(sV, m_v1), (sS, m_s1)], inc=sV)
    m_s2 = p.op("scalar", lambda sc: sc.activation(
        stat[:, 7:8], stat[:, 6:7], AF.Sqrt, bias=float(EPS)),
        waits=[(sV, m_v2)], inc=sS)

    def f_v3(v):
        v.reciprocal(stat[:, 8:9], stat[:, 7:8])
        v.drain()
        v.tensor_tensor(stat[:, 9:10], stat[:, 3:4], stat[:, 8:9], ALU.mult)
        v.drain()
        v.tensor_scalar(stat[:, 10:11], stat[:, 9:10], -1.0, None, ALU.mult)
        v.drain()
        return v.tensor_scalar(out[:, outslice[0]:outslice[1]],
                               src[:, s0:s0 + 1024],
                               stat[:, 8:9], stat[:, 10:11], ALU.mult, ALU.add)
    return p.op("vector", f_v3, waits=[(sS, m_s2)], inc=sV)


def phase1_seq(nc, p, seq, L):
    sD, sG, sT, sV, sS, sP = L["sD"], L["sG"], L["sT"], L["sV"], L["sS"], L["sP"]
    dx, d_cp = L["dx"], L["d_cp"]
    identb, identf, cvec = L["identb"], L["identf"], L["cvec"]
    stat, xtok, xnf32 = L["stat"], L["xtok"], L["xnf32"]
    wdisp, wval, wcproj, sdw = L["wdisp"], L["wval"], L["wcproj"], L["sdw"]
    xntok, xnfm, dwseq = L["xntok"], L["xnfm"], L["dwseq"]
    dispfm, valfm, valtok, attntok = L["dispfm"], L["valfm"], L["valtok"], L["attntok"]
    attnfm, cptile = L["attnfm"], L["cptile"]
    wts_G = L["wts_G"]

    prev_tr = 0
    for blk in range(NBLK):
        waits = [(sD, prev_tr)] if prev_tr else []
        m_load = p.op("sync", lambda s, seq=seq, blk=blk: s.dma_start(
            xtok[:, :], bass.AP(dx, (seq * T + blk * 128) * C, [[C, 128], [1, C]])),
            waits=waits, inc=sD, n=16)
        m_xn = ln_stats_apply(nc, p, L, xtok, (0, 1024), xnf32, (0, 1024),
                              [(sD, m_load)], scratch=xnf32)
        m_cp = p.op("vector", lambda v: v.tensor_copy(xntok[:, :], xnf32[:, :]),
                    waits=([(sD, prev_tr)] if prev_tr else []), inc=sV)
        for cc in range(8):
            p.op("sync", lambda s, cc=cc, blk=blk: s.dma_start(
                bass.AP(xnfm, cc * 2048 + blk * 128, [[8 * 2048, 128], [1, 128]]),
                bass.AP(xntok, cc * 128, [[1024, 128], [1, 128]]),
                transpose=True),
                waits=([(sV, m_cp)] if cc == 0 else []), inc=sD, n=16)
        prev_tr = p.val(sD)
    xnfm_D = p.val(sD)
    p.ckpt(f"p1a_{seq}")

    def f_init(v):
        for g4 in range(4):
            v.memset(dwseq[:, g4 * 2080:g4 * 2080 + 32], 0.0)
        v.memset(valtok[:, 0:1040], 0.0)
        v.drain()
        return v.memset(bass.AP(valtok, 64, [[VTS, 128], [1040, 17], [65, 16]]), 1.0)
    init_V = p.op("vector", f_init, waits=[(sT, p.val(sT))], inc=sV)

    for tw in range(4):
        with nc.psum_tensor(f"pp_{seq}_{tw}", [128, 512], F32) as pp, \
             nc.psum_tensor(f"pvt_{seq}_{tw}", [128, 2048], BF16) as pvt:
            prev_evac = 0
            for m in range(2):
                def f_disp(t, m=m, tw=tw):
                    for k in range(8):
                        inst = t.matmul(
                            pp[:, :],
                            bass.AP(wdisp, k * 256 + m * 128, [[8 * 256, 128], [1, 128]]),
                            bass.AP(xnfm, k * 2048 + tw * 512, [[8 * 2048, 128], [1, 512]]),
                            start=(k == 0), stop=(k == 7))
                    return inst
                w = []
                if tw == 0 and m == 0:
                    w = [(sD, xnfm_D), (sG, wts_G)]
                if prev_evac:
                    w.append((sS, prev_evac))
                m_t = p.op("tensor", f_disp, waits=w, inc=sT)
                prev_evac = p.op("scalar", lambda sc, m=m: sc.copy(
                    dispfm[:, m * 512:(m + 1) * 512], pp[:, :]),
                    waits=[(sT, m_t)], inc=sS)
            disp_S = prev_evac

            for g4 in range(4):
                dgrp, half = g4 // 2, g4 % 2
                m_t = p.op("tensor", lambda t, dgrp=dgrp, half=half: t.matmul(
                    pp[:, :], sdw[:, half * 128:(half + 1) * 128],
                    dispfm[:, dgrp * 512:(dgrp + 1) * 512],
                    start=True, stop=True),
                    waits=[(sS, prev_evac)], inc=sT)
                prev_evac = p.op("scalar", lambda sc, g4=g4, tw=tw: sc.copy(
                    dwseq[:, g4 * 2080 + 32 + tw * 512: g4 * 2080 + 32 + (tw + 1) * 512],
                    pp[:, :]), waits=[(sT, m_t)], inc=sS)

            for m in range(8):
                def f_val(t, m=m, tw=tw):
                    for k in range(8):
                        inst = t.matmul(
                            pp[:, :],
                            bass.AP(wval, k * 1024 + m * 128, [[8 * 1024, 128], [1, 128]]),
                            bass.AP(xnfm, k * 2048 + tw * 512, [[8 * 2048, 128], [1, 512]]),
                            start=(k == 0), stop=(k == 7))
                    return inst
                m_t = p.op("tensor", f_val, waits=[(sS, prev_evac)], inc=sT)
                prev_evac = p.op("scalar", lambda sc, m=m: sc.copy(
                    valfm[:, m * 512:(m + 1) * 512], pp[:, :]),
                    waits=[(sT, m_t)], inc=sS)
            val_S = prev_evac

            prev_vte = 0
            for blkw in range(4):
                blk = tw * 4 + blkw
                vmarks = {}
                for vg in range(8):
                    w = []
                    if vg == 0:
                        w = [(sS, val_S), (sV, init_V)]
                        if prev_vte:
                            w.append((sV, prev_vte))
                    if vg >= 2:
                        w.append((sT, vmarks[vg - 2]))
                    slot = (vg % 2) * 1024 + (vg // 2) * 128
                    vmarks[vg] = p.op("tensor", lambda t, vg=vg, blkw=blkw, slot=slot: t.transpose(
                        pvt[:, slot:slot + 128],
                        bass.AP(valfm, vg * 512 + blkw * 128, [[8 * 512, 128], [1, 128]]),
                        identb[:, :]), waits=w, inc=sT)
                vt_T = p.val(sT)
                for vgpar in range(2):
                    prev_vte = p.op("vector", lambda v, blk=blk, vgpar=vgpar: v.tensor_copy(
                        bass.AP(valtok, (blk + 1) * 1040 + vgpar * 130,
                                [[VTS, 128], [260, 4], [65, 2], [1, 64]]),
                        bass.AP(pvt, vgpar * 1024, [[2048, 128], [128, 4], [64, 2], [1, 64]])),
                        waits=[(sT, vt_T)] if vgpar == 0 else [], inc=sV)
            vt_V = prev_vte

        p.ckpt(f"p1b_{seq}_{tw}")
        attn_core_tw(nc, p, seq, tw, L, vt_V)
        p.ckpt(f"p1c_{seq}_{tw}")

        with nc.psum_tensor(f"pcp_{seq}_{tw}", [128, 128], BF16) as pcp, \
             nc.psum_tensor(f"pcj_{seq}_{tw}", [128, 512], F32) as pcj:
            prev_ev = 0
            for cc in range(8):
                for blkw in range(4):
                    w = [(sV, prev_ev)] if prev_ev else []
                    if cc == 0 and blkw == 0:
                        w.append((sS, p.val(sS)))
                    m_t = p.op("tensor", lambda t, cc=cc, blkw=blkw: t.transpose(
                        pcp[:, :],
                        bass.AP(attntok, blkw * 1024 + cc * 128, [[4 * 1024, 128], [1, 128]]),
                        identb[:, :]), waits=w, inc=sT)
                    prev_ev = p.op("vector", lambda v, cc=cc, blkw=blkw: v.tensor_copy(
                        attnfm[:, cc * 512 + blkw * 128: cc * 512 + (blkw + 1) * 128],
                        pcp[:, :]), waits=[(sT, m_t)], inc=sV)
            afm_V = prev_ev
            prev_dma = 0
            for m in range(8):
                def f_cp(t, m=m):
                    for k in range(8):
                        inst = t.matmul(
                            pcj[:, :],
                            bass.AP(wcproj, k * 1024 + m * 128, [[8 * 1024, 128], [1, 128]]),
                            bass.AP(attnfm, k * 512, [[8 * 512, 128], [1, 512]]),
                            start=(k == 0), stop=(k == 7))
                    return inst
                w = [(sV, afm_V)] if m == 0 else []
                if prev_dma:
                    w.append((sS, prev_me))
                m_t = p.op("tensor", f_cp, waits=w, inc=sT)
                ew = [(sT, m_t), (sD, p.val(sD))]
                prev_me = p.op("scalar", lambda sc: sc.copy(cptile[:, :], pcj[:, :]),
                               waits=ew, inc=sS)
                prev_dma = p.op("sync", lambda s, seq=seq, m=m, tw=tw: s.dma_start(
                    bass.AP(d_cp, ((seq * 8 + m) * 128) * T + tw * 512, [[T, 128], [1, 512]]),
                    cptile[:, :]), waits=[(sS, prev_me)], inc=sD, n=16)


def attn_core_tw(nc, p, seq, tw, L, vt_V):
    sD, sG, sT, sV, sS, sP = L["sD"], L["sG"], L["sT"], L["sV"], L["sS"], L["sP"]
    identb, identf, cbias, cvec = L["identb"], L["identf"], L["cbias"], L["cvec"]
    sdot, dwseq = L["sdot"], L["dwseq"]
    valtok, attntok = L["valtok"], L["attntok"]
    ebuf, gbuf, sigb, bondb, stok = L["ebuf"], L["gbuf"], L["sigb"], L["bondb"], L["stok"]
    stok2 = L["stok2"]
    expw, rot, tdd, tee, dd, ee, rz = (L["expw"], L["rot"], L["tdd"], L["tee"],
                                       L["dd"], L["ee"], L["rz"])
    sumA, sumB = L["sumA"], L["sumB"]
    cbias_V = L["cbias_V"]

    with nc.psum_tensor(f"pjlo_{seq}_{tw}", [128, 512], F32) as pjlo, \
         nc.psum_tensor(f"pjhi_{seq}_{tw}", [128, 512], F32) as pjhi, \
         nc.psum_tensor(f"psand_{seq}_{tw}", [128, 1024], F32) as psand:
        exp_marks = []
        for g in range(4):
            dwoff = g * 2080 + 32 + tw * 512
            g_marks = [0, 0]
            mm_marks = [0, 0]
            for j in range(32):
                par = j % 2
                jh, jj = j // 16, j % 16
                pj = pjlo if jh == 0 else pjhi
                waits = []
                if j == 0 and g == 0:
                    waits += [(sS, p.val(sS))]
                    if tw == 0 and seq == 0:
                        waits += [(sV, cbias_V)]
                if g_marks[par]:
                    waits.append((sS, g_marks[par]))
                m_e = p.op("vector", lambda v, dwoff=dwoff, j=j, par=par: v.tensor_tensor(
                    ebuf[:, par * 512:(par + 1) * 512],
                    bass.AP(dwseq, dwoff + (j - 31), [[4 * 2080, 128], [1, 512]]),
                    bass.AP(dwseq, dwoff, [[4 * 2080, 128], [1, 512]]),
                    ALU.subtract), waits=waits, inc=sV)
                gw = [(sV, m_e)]
                if mm_marks[par]:
                    gw.append((sT, mm_marks[par]))
                m_g = p.op("scalar", lambda sc, j=j, par=par: sc.activation(
                    gbuf[:, par * 512:(par + 1) * 512],
                    ebuf[:, par * 512:(par + 1) * 512],
                    AF.Gelu, bias=cbias[:, j:j + 1]), waits=gw, inc=sS)
                g_marks[par] = m_g
                m_t = p.op("tensor", lambda t, jj=jj, par=par, pj=pj: t.matmul(
                    pj[:, :], sdot[:, jj * 128:(jj + 1) * 128],
                    gbuf[:, par * 512:(par + 1) * 512],
                    start=(jj == 0), stop=(jj == 15)),
                    waits=[(sS, m_g)], inc=sT)
                mm_marks[par] = m_t
            dots_T = p.val(sT)
            if g == 0:
                p.ckpt(f"jloop_{seq}_{tw}")

            for jh in range(2):
                pj = pjlo if jh == 0 else pjhi
                p.op("scalar", lambda sc, pj=pj, jh=jh: sc.activation(
                    bass.AP(sigb, 64 * 1024 + jh * 512,
                            [[1024, 64], [128, 4], [1, 4], [4, 32]]),
                    pj[64:128, :], AF.Sigmoid, bias=cvec[64:128, 1:2]),
                    waits=[(sT, dots_T)], inc=sS)
                p.op("vector", lambda v, pj=pj, jh=jh: v.tensor_copy(
                    bass.AP(bondb, jh * 512, [[1024, 64], [128, 4], [1, 4], [4, 32]]),
                    pj[0:64, :]), waits=[(sT, dots_T)], inc=sV)
            sig_S, bond_V = p.val(sS), p.val(sV)
            if g == 0:
                p.ckpt(f"sig_{seq}_{tw}")

            for jh in range(2):
                for blkw in range(4):
                    m_ta = p.op("tensor", lambda t, jh=jh, blkw=blkw: t.transpose(
                        psand[:, 0:64],
                        bass.AP(bondb, jh * 512 + blkw * 128, [[1024, 64], [1, 128]]),
                        identf[0:64, 0:64]),
                        waits=[(sS, sig_S), (sV, p.val(sV))], inc=sT)
                    if g == 0 and jh == 0 and blkw == 0:
                        p.ckpt(f"tA_{seq}_{tw}")
                    m_tb = p.op("tensor", lambda t, jh=jh, blkw=blkw: t.transpose(
                        psand[:, 512:576],
                        bass.AP(sigb, 64 * 1024 + jh * 512 + blkw * 128, [[1024, 64], [1, 128]]),
                        identf[64:128, 64:128]), inc=sT)
                    if g == 0 and jh == 0 and blkw == 0:
                        p.ckpt(f"tB_{seq}_{tw}")
                    m_cb0 = p.op("scalar", lambda sc: sc.copy(
                        stok2[:, :], psand[:, 0:64]),
                        waits=[(sT, m_tb), (sV, p.val(sV))], inc=sS)
                    m_cb = p.op("vector", lambda v: v.scalar_tensor_tensor(
                        stok[:, :], psand[:, 512:576], -10.0, stok2[:, :],
                        ALU.mult, ALU.add),
                        waits=[(sT, m_tb), (sS, m_cb0)], inc=sV)
                    if g == 0 and jh == 0 and blkw == 0:
                        p.ckpt(f"comb_{seq}_{tw}")
                    m_ex = p.op("scalar", lambda sc, g=g, jh=jh, blkw=blkw: sc.activation(
                        bass.AP(expw, blkw * 1024 + g * 256 + jh * 16,
                                [[4 * 1024, 128], [64, 4], [1, 16]]),
                        stok[:, :], AF.Exp), waits=[(sV, m_cb)], inc=sS)
                    exp_marks.append(m_ex)
                    if g == 0 and jh == 0 and blkw == 0:
                        p.ckpt(f"sand1_{seq}_{tw}")
        exp_S = p.val(sS)
        p.ckpt(f"dots_{seq}_{tw}")
        mask0 = L["mask0"]
        for blkw in range(4):
            blk = tw * 4 + blkw
            base = blkw * 1024
            m_dup = p.op("vector", lambda v, base=base: v.tensor_copy(
                bass.AP(expw, base + 32, [[4096, 128], [64, 16], [1, 32]]),
                bass.AP(expw, base, [[4096, 128], [64, 16], [1, 32]])),
                waits=[(sS, exp_S)], inc=sV)
            if blk == 0:
                m_dup = p.op("vector", lambda v, base=base: v.tensor_tensor(
                    expw[:, base:base + 1024], expw[:, base:base + 1024],
                    mask0[:, :], ALU.mult), waits=[(sV, m_dup)], inc=sV)

            skm = L["skm"]
            m_z = p.op("vector", lambda v: v.memset(rot[:, :], 0.0),
                       waits=[(sV, m_dup), (sT, p.val(sT))], inc=sV)
            for s_ in range(32):
                p.op("vector", lambda v, base=base, s_=s_: v.scalar_tensor_tensor(
                    rot[:, :],
                    bass.AP(expw, base + (31 - s_), [[4096, 128], [64, 16], [1, 32]]),
                    skm[:, s_:s_ + 1], rot[:, :], ALU.mult, ALU.add),
                    waits=[(sV, m_z)] if s_ == 0 else [], inc=sV)
            skew_V = p.val(sV)
            if blkw == 0:
                p.ckpt(f"skew_{seq}_{tw}")

            with nc.psum_tensor(f"prot_{seq}_{tw}_{blkw}", [128, 2048], BF16) as prot, \
                 nc.psum_tensor(f"pout_{seq}_{tw}_{blkw}", [128, 1024], F32) as pout:
                hmarks = {}
                for W in range(4):
                    for h in range(16):
                        w = []
                        if W == 0 and h == 0:
                            w = [(sV, skew_V), (sP, p.val(sP))]
                        if h >= 2:
                            w.append((sT, hmarks[(W, h - 2)]))
                        elif W > 0:
                            w.append((sT, hmarks[(W - 1, 14 + h)]))
                        slot = (h % 2) * 1024 + (h // 2) * 128
                        hmarks[(W, h)] = p.op("tensor", lambda t, W=W, h=h, slot=slot: t.transpose(
                            prot[32 * W:32 * W + 32, slot:slot + 128],
                            rot[:, h * 32:(h + 1) * 32], identb[:, :],
                            tile_position=(0, 32 * W)), waits=w, inc=sT)
                    rotT_T = p.val(sT)
                    m_ed = p.op("vector", lambda v, W=W: v.tensor_copy(
                        bass.AP(tdd, (32 * W) * 512, [[512, 32], [32, 2], [64, 8], [1, 32]]),
                        bass.AP(prot, (32 * W) * 2048 + W, [[2048, 32], [1024, 2], [128, 8], [4, 32]])),
                        waits=[(sT, rotT_T), (sP, p.val(sP))], inc=sV)
                    m_ee2 = p.op("vector", lambda v, W=W: v.tensor_copy(
                        bass.AP(tee, (32 * W) * 512, [[512, 32], [32, 2], [64, 8], [1, 32]]),
                        bass.AP(prot, (32 * W) * 2048 + (W + 1) % 4,
                                [[2048, 32], [1024, 2], [128, 8], [4, 32]])), inc=sV)
                    p.op("gpsimd", lambda g, W=W: g.affine_select(
                        dd[32 * W:32 * W + 32, :], tdd[32 * W:32 * W + 32, :],
                        pattern=[[0, 16], [1, 32]], compare_op=ALU.is_ge,
                        fill=0.0, channel_multiplier=-1),
                        waits=[(sV, m_ee2), (sT, rotT_T)], inc=sP)
                    p.op("gpsimd", lambda g, W=W: g.affine_select(
                        ee[32 * W:32 * W + 32, :], tee[32 * W:32 * W + 32, :],
                        pattern=[[0, 16], [-1, 32]], compare_op=ALU.is_ge,
                        fill=0.0, base=-1, channel_multiplier=1), inc=sP)
                sel_P = p.val(sP)
                if blkw == 0:
                    p.ckpt(f"sel_{seq}_{tw}")

                bankmark = {}
                for hb in range(4):
                    if hb == 0:
                        bw = [(sP, sel_P), (sV, vt_V), (sS, p.val(sS)), (sV, p.val(sV))]
                    else:
                        bw = [(sS, p.val(sS)), (sV, p.val(sV))]
                    for hh in range(4):
                        h = hb * 4 + hh
                        bA = hh % 2
                        slotA = bA * 512 + (hh // 2) * 65
                        bB = 1 - bA
                        slotB = bB * 512 + (hh // 2) * 65 + 130
                        wt0 = list(bw) if hh == 0 else []
                        for V in range(4):
                            w = list(wt0) if V == 0 else []
                            key = (V, bA)
                            if key in bankmark:
                                w.append((sT, bankmark[key]))
                            bankmark[key] = p.op("tensor", lambda t, h=h, V=V, slotA=slotA, blk=blk: t.matmul(
                                pout[32 * V:32 * V + 32, slotA:slotA + 65],
                                dd[32 * V:32 * V + 32, h * 32:(h + 1) * 32],
                                bass.AP(valtok, (32 * V) * VTS + (blk + 1) * 1040 + h * 65,
                                        [[VTS, 32], [1, 65]]),
                                start=True, stop=True,
                                tile_position=(32 * V, 32 * V)),
                                waits=w, inc=sT)
                        for V in range(4):
                            q = 0 if V == 0 else V - 1
                            outbase = 0 if V == 0 else 32 * V
                            key = (outbase // 32, bB)
                            w = []
                            if key in bankmark:
                                w.append((sT, bankmark[key]))
                            if V == 0:
                                bankmark[key] = p.op("tensor", lambda t, h=h, slotB=slotB, blk=blk: t.matmul(
                                    pout[0:32, slotB:slotB + 65],
                                    ee[96:128, h * 32:(h + 1) * 32],
                                    bass.AP(valtok, 96 * VTS + blk * 1040 + h * 65,
                                            [[VTS, 32], [1, 65]]),
                                    start=True, stop=True,
                                    tile_position=(96, 0)), waits=w, inc=sT)
                            else:
                                bankmark[key] = p.op("tensor", lambda t, h=h, V=V, slotB=slotB, blk=blk: t.matmul(
                                    pout[32 * V:32 * V + 32, slotB:slotB + 65],
                                    ee[32 * (V - 1):32 * V, h * 32:(h + 1) * 32],
                                    bass.AP(valtok, (32 * (V - 1)) * VTS + (blk + 1) * 1040 + h * 65,
                                            [[VTS, 32], [1, 65]]),
                                    start=True, stop=True,
                                    tile_position=(32 * (V - 1), 32 * V)), waits=w, inc=sT)
                    band_T = p.val(sT)
                    if blkw == 0 and hb == 0:
                        p.ckpt(f"band_{seq}_{tw}")
                    for hh in range(4):
                        h = hb * 4 + hh
                        bA = hh % 2
                        slotA = bA * 512 + (hh // 2) * 65
                        slotB = (1 - bA) * 512 + (hh // 2) * 65 + 130
                        m_ca = p.op("scalar", lambda sc, slotA=slotA: sc.copy(
                            sumA[:, :], pout[:, slotA:slotA + 65]),
                            waits=[(sT, band_T)] if hh == 0 else [], inc=sS)
                        m_sm = p.op("vector", lambda v, slotB=slotB: v.scalar_tensor_tensor(
                            sumB[:, :], pout[:, slotB:slotB + 65], 1.0, sumA[:, :],
                            ALU.mult, ALU.add), waits=[(sS, m_ca)], inc=sV)
                        m_r = p.op("vector", lambda v, h=h: v.reciprocal(
                            rz[:, h:h + 1], sumB[:, 64:65]), waits=[(sV, m_sm)], inc=sV)
                        p.op("scalar", lambda sc, h=h, blkw=blkw: sc.activation(
                            attntok[:, blkw * 1024 + h * 64: blkw * 1024 + (h + 1) * 64],
                            sumB[:, 0:64], AF.Identity,
                            scale=rz[:, h:h + 1]), waits=[(sV, m_r)], inc=sS)


def quantize_weights(nc, p, L):
    sD, sG, sT, sV, sS = L["sD"], L["sG"], L["sT"], L["sV"], L["sS"]
    d_w1t, d_w2t = L["d_w1t"], L["d_w2t"]
    w1q, w2q, scol, ucol, vcol, ubias = (
        L["w1q"], L["w2q"], L["scol"], L["ucol"], L["vcol"], L["ubias"])
    b2lnc, b2lnb, b2fc, g2c = L["b2lnc"], L["b2lnb"], L["b2fc"], L["g2c"]
    onescf, onescb, onesrow, identf = L["onescf"], L["onescb"], L["onesrow"], L["identf"]
    qrow = L["qrow"]

    p.op("vector", lambda v: v.tensor_copy(b2lnb[:, :], b2lnc[:, :]), inc=sV)

    from contextlib import ExitStack
    with ExitStack() as _sq:
        qscr = _sq.enter_context(nc.sbuf_tensor("qscr", [128, 4096], F32))
        qm = _sq.enter_context(nc.sbuf_tensor("qm", [128, 4096], F32))
        qsg = _sq.enter_context(nc.sbuf_tensor("qsg", [128, 4096], F32))
        qpart = _sq.enter_context(nc.sbuf_tensor("qpart", [128, 40], F32))
        pq = _sq.enter_context(nc.psum_tensor("pq", [128, 8], F32))
        pq1 = _sq.enter_context(nc.psum_tensor("pq1", [1, 256], F32))

        for wi, (dtens, nch, chw, qdst, scoli) in enumerate(
                [(d_w1t, 8, 4096, w1q, 0), (d_w2t, 32, 1024, w2q, 2)]):
            prev_rd = 0
            for k in range(nch):
                wts = [(sV, prev_rd)] if prev_rd else [(sV, p.val(sV))]
                m_l = p.op("sync", lambda s, dtens=dtens, k=k, chw=chw: s.dma_start(
                    qscr[:, 0:chw],
                    bass.AP(dtens, k * 128 * chw, [[chw, 128], [1, chw]])),
                    waits=wts, inc=sD, n=16)
                prev_rd = p.op("vector", lambda v, k=k, chw=chw: v.tensor_reduce(
                    qpart[:, k:k + 1], qscr[:, 0:chw], AX.X, ALU.add,
                    apply_absolute_value=True), waits=[(sD, m_l)], inc=sV)
            m_r2 = p.op("vector", lambda v, nch=nch: v.tensor_reduce(
                qpart[:, 36:37], qpart[:, 0:nch], AX.X, ALU.add), inc=sV)
            m_t = p.op("tensor", lambda t: t.matmul(
                pq1[0:1, 0:1], onescf[:, 0:1], qpart[:, 36:37], start=True, stop=True),
                waits=[(sV, m_r2)], inc=sT)

            def f_scal(v, nch=nch, chw=chw):
                v.tensor_scalar(qrow[0:1, 0:1], pq1[0:1, 0:1],
                                1.0 / (nch * 128 * chw), None, ALU.mult)
                v.drain()
                v.tensor_scalar(qrow[0:1, 0:1], qrow[0:1, 0:1], 1e-5, None, ALU.max)
                v.drain()
                return v.reciprocal(qrow[0:1, 1:2], qrow[0:1, 0:1])
            m_v = p.op("vector", f_scal, waits=[(sT, m_t)], inc=sV)
            m_t2 = p.op("tensor", lambda t: t.matmul(
                pq[:, 0:1], onesrow[0:1, :], qrow[0:1, 1:2], start=True, stop=True),
                waits=[(sV, m_v)], inc=sT)
            m_t3 = p.op("tensor", lambda t: t.matmul(
                pq[:, 1:2], onesrow[0:1, :], qrow[0:1, 0:1], start=True, stop=True),
                waits=[(sT, m_t2)], inc=sT)
            scol_V = p.op("vector", lambda v, scoli=scoli: v.tensor_copy(
                scol[:, scoli:scoli + 2], pq[:, 0:2]), waits=[(sT, m_t3)], inc=sV)

            for k in range(nch):
                m_l = p.op("sync", lambda s, dtens=dtens, k=k, chw=chw: s.dma_start(
                    qscr[:, 0:chw],
                    bass.AP(dtens, k * 128 * chw, [[chw, 128], [1, chw]])),
                    waits=[(sV, p.val(sV))], inc=sD, n=16)

                def f_q(v, chw=chw, scoli=scoli):
                    v.tensor_scalar(qscr[:, 0:chw], qscr[:, 0:chw],
                                    scol[:, scoli:scoli + 1], None, ALU.mult)
                    v.drain()
                    v.scalar_tensor_tensor(qm[:, 0:chw], qscr[:, 0:chw], -1.0,
                                           qscr[:, 0:chw], ALU.mult, ALU.max)
                    v.drain()
                    return v.tensor_scalar(qm[:, 0:chw], qm[:, 0:chw], 0.5, None, ALU.is_ge)
                m_q = p.op("vector", f_q, waits=[(sD, m_l), (sV, scol_V)], inc=sV)
                m_sg2 = p.op("scalar", lambda sc, chw=chw: sc.activation(
                    qsg[:, 0:chw], qscr[:, 0:chw], AF.Sign), waits=[(sV, m_q)], inc=sS)
                p.op("vector", lambda v, qdst=qdst, k=k, chw=chw: v.tensor_tensor(
                    qdst[:, k * chw:(k + 1) * chw], qm[:, 0:chw], qsg[:, 0:chw], ALU.mult),
                    waits=[(sS, m_sg2)], inc=sV)
        q_done_V = p.val(sV)

        for m in range(8):
            def f_uc(t, m=m):
                for k in range(32):
                    inst = t.matmul(pq[:, 2:3],
                                    w2q[:, k * 1024 + m * 128: k * 1024 + (m + 1) * 128],
                                    b2lnb[:, k:k + 1], start=(k == 0), stop=(k == 31))
                return inst
            w = [(sV, q_done_V)] if m == 0 else [(sV, p.val(sV))]
            m_t = p.op("tensor", f_uc, waits=w, inc=sT)
            p.op("vector", lambda v, m=m: v.tensor_copy(ucol[:, m:m + 1], pq[:, 2:3]),
                 waits=[(sT, m_t)], inc=sV)
        uc_V = p.val(sV)

        for k in range(32):
            p.op("vector", lambda v, k=k: v.tensor_scalar(
                w2q[:, k * 1024:(k + 1) * 1024], w2q[:, k * 1024:(k + 1) * 1024],
                g2c[:, k:k + 1], None, ALU.mult),
                waits=[(sV, uc_V), (sT, p.val(sT))] if k == 0 else [], inc=sV)
        fold_V = p.val(sV)

        for m in range(8):
            def f_vc(t, m=m):
                for k in range(32):
                    inst = t.matmul(pq[:, 3:4],
                                    w2q[:, k * 1024 + m * 128: k * 1024 + (m + 1) * 128],
                                    onescb[:, 0:1], start=(k == 0), stop=(k == 31))
                return inst
            w = [(sV, fold_V)] if m == 0 else [(sV, p.val(sV))]
            m_t = p.op("tensor", f_vc, waits=w, inc=sT)
            p.op("vector", lambda v, m=m: v.tensor_scalar(
                vcol[:, m:m + 1], pq[:, 3:4], scol[:, 3:4], None, ALU.mult),
                waits=[(sT, m_t)], inc=sV)

        def f_ub(v):
            v.tensor_scalar(ubias[:, 0:8], ucol[:, 0:8], scol[:, 3:4], None, ALU.mult)
            v.drain()
            return v.tensor_tensor(ubias[:, 0:8], ubias[:, 0:8], b2fc[:, 0:8], ALU.add)
        p.op("vector", f_ub, waits=[(sV, p.val(sV))], inc=sV)


def phase2_front(nc, p, tile, L):
    sD, sG, sT, sV, sS = L["sD"], L["sG"], L["sT"], L["sV"], L["sS"]
    sX = L["sX"]
    dx, d_cp = L["dx"], L["d_cp"]
    ln1gb, ln1bb = L["ln1gb"], L["ln1bb"]
    xtok2, xn2, ln1tok, ln1fm = L["xtok2"], L["xn2"], L["ln1tok"], L["ln1fm"]
    cptw = L["cptw"]
    p2m = L.setdefault("p2m", {})

    seq, twl = tile // 8, tile % 8
    tok0 = twl * TW2
    par3 = tile % 3
    xb = par3 * 2048
    cb = (tile % 2) * 2048

    wx = [(sV, p2m[("x", par3)])] if ("x", par3) in p2m else []
    m_lx = p.op("sync", lambda s: s.dma_start(
        bass.AP(xtok2, xb, [[3 * 2048, 128], [1024, 2], [1, 1024]]),
        bass.AP(dx, (seq * T + tok0) * C, [[C, 128], [128 * C, 2], [1, C]])),
        waits=wx, inc=sX, n=16)
    wc = [(sV, p2m["outV"])] if "outV" in p2m else []
    m_lc = p.op("gpsimd", lambda g: g.dma_start(
        bass.AP(cptw, cb, [[2 * 8 * 256, 128], [256, 8], [1, 256]]),
        bass.AP(d_cp, (seq * 8) * 128 * T + tok0, [[T, 128], [128 * T, 8], [1, 256]])),
        waits=wc, inc=sG, n=16)

    lb = (tile % 2) * 2048
    w1pred = p2m.get("sTend", 0)
    for tb in range(2):
        o = tb * 1024
        m_xn = ln_stats_apply(nc, p, L, xtok2, (xb + o, xb + o + 1024),
                              xn2, (0, 1024), [(sX, m_lx)])
        m_l1 = ln_stats_apply(nc, p, L, xn2, (0, 1024), ln1tok, (0, 1024),
                              [(sV, m_xn), (sD, 1)])
        for cc in range(8):
            w = []
            if cc == 0:
                w = [(sV, m_l1)]
                if w1pred:
                    w.append((sT, w1pred))
            p.op("sync", lambda s, cc=cc, tb=tb: s.dma_start(
                bass.AP(ln1fm, lb + cc * 256 + tb * 128, [[2 * 8 * 256, 128], [1, 128]]),
                bass.AP(ln1tok, cc * 128, [[1024, 128], [1, 128]]),
                transpose=True), waits=w, inc=sD, n=16)
    lnfm_D = p.val(sD)

    for cc in range(8):
        p.op("vector", lambda v, cc=cc: v.tensor_scalar(
            ln1fm[:, lb + cc * 256:lb + (cc + 1) * 256],
            ln1fm[:, lb + cc * 256:lb + (cc + 1) * 256],
            ln1gb[:, cc:cc + 1], ln1bb[:, cc:cc + 1], ALU.mult, ALU.add),
            waits=[(sD, lnfm_D)] if cc == 0 else [], inc=sV)
    p2m[("ln1", tile)] = p.val(sV)
    p2m[("lc", par3)] = m_lc


def phase2_back(nc, p, tile, L):
    sD, sG, sT, sV, sS, sO = L["sD"], L["sG"], L["sT"], L["sV"], L["sS"], L["sO"]
    dout = L["dout"]
    identf = L["identf"]
    w1q, w2q, scol, ubias, vcol = L["w1q"], L["w2q"], L["scol"], L["ubias"], L["vcol"]
    b1c = L["b1c"]
    onescb, onesrow = L["onescb"], L["onesrow"]
    xtok2, ln1fm = L["xtok2"], L["ln1fm"]
    h1g, sqt, rrow = L["h1g"], L["sqt"], L["rrow"]
    rrep, murep = L["rrep"], L["murep"]
    cptw, sumt, outtok = L["cptw"], L["sumt"], L["outtok"]
    p2m = L["p2m"]

    seq, twl = tile // 8, tile % 8
    tok0 = twl * TW2
    par3 = tile % 3
    xb = par3 * 2048
    cb = (tile % 2) * 2048
    lb = (tile % 2) * 2048
    ln1_V = p2m[("ln1", tile)]
    m_lc = p2m[("lc", par3)]

    with nc.psum_tensor(f"ph1a_{tile}", [128, 256], F32) as ph1a, \
         nc.psum_tensor(f"ph1b_{tile}", [128, 256], F32) as ph1b, \
         nc.psum_tensor(f"pz1_{tile}", [1, 256], F32) as pz1, \
         nc.psum_tensor(f"pz2_{tile}", [1, 256], F32) as pz2, \
         nc.psum_tensor(f"pR_{tile}", [128, 256], F32) as pR, \
         nc.psum_tensor(f"pMu_{tile}", [128, 256], F32) as pMu, \
         nc.psum_tensor(f"po_{tile}", [128, 256], F32) as po:
        ph1s = (ph1a, ph1b)
        pft = ph1a
        gel = {}
        sqm = {}

        def emit_z(mm):
            zw = [(sS, gel[mm])]
            if mm == 0 and "statsV" in p2m:
                zw.append((sV, p2m["statsV"]))
            p.op("tensor", lambda t, mm=mm: t.matmul(
                pz1[0:1, 0:256], onescb[:, 0:1], h1g[:, mm * 256:(mm + 1) * 256],
                start=(mm == 0), stop=(mm == 31)), waits=zw, inc=sT)
            sqw2 = [(sS, sqm[mm])]
            if mm == 0 and "statsV" in p2m:
                sqw2.append((sV, p2m["statsV"]))
            return p.op("tensor", lambda t, mm=mm: t.matmul(
                pz2[0:1, 0:256], onescb[:, 0:1], sqt[:, :],
                start=(mm == 0), stop=(mm == 31)), waits=sqw2, inc=sT)

        sq_mm = 0
        for m in range(32):
            def f_w1(t, m=m):
                ph1 = ph1s[m % 2]
                for k in range(8):
                    inst = t.matmul(ph1[:, :],
                                    w1q[:, k * 4096 + m * 128: k * 4096 + (m + 1) * 128],
                                    ln1fm[:, lb + k * 256:lb + (k + 1) * 256],
                                    start=(k == 0), stop=(k == 7))
                return inst
            w = []
            if m == 0:
                w = [(sV, max(p2m.get("outV", 0), ln1_V))]
            if m < 2:
                if ("ph1S", m % 2) in p2m:
                    w.append((sS, p2m[("ph1S", m % 2)]))
            else:
                w = [(sS, gel[m - 2])]
            m_t = p.op("tensor", f_w1, waits=w, inc=sT)
            gel[m] = p.op("scalar", lambda sc, m=m: sc.activation(
                h1g[:, m * 256:(m + 1) * 256], ph1s[m % 2][:, :], AF.Gelu,
                bias=b1c[:, m:m + 1], scale=scol[:, 1:2]), waits=[(sT, m_t)], inc=sS)
            if m >= 1:
                sq_mm = emit_z(m - 1)
            sqw = [(sS, gel[m])]
            if sq_mm:
                sqw.append((sT, sq_mm))
            sqm[m] = p.op("scalar", lambda sc, m=m: sc.activation(
                sqt[:, :], h1g[:, m * 256:(m + 1) * 256], AF.Square), waits=sqw, inc=sS)
        emit_z(31)
        z_T = p.val(sT)
        prev_g = gel[31]
        p2m[("ph1S", 0)] = gel[30]
        p2m[("ph1S", 1)] = gel[31]

        def f_stats(v):
            v.tensor_scalar(rrow[0:1, 256:512], pz1[0:1, 0:256], 1.0 / HID, None, ALU.mult)
            v.tensor_scalar(rrow[0:1, 0:256], pz2[0:1, 0:256], 1.0 / HID, None, ALU.mult)
            v.drain()
            v.tensor_tensor(rrow[0:1, 512:768], rrow[0:1, 256:512],
                            rrow[0:1, 256:512], ALU.mult)
            v.drain()
            return v.tensor_tensor(rrow[0:1, 512:768], rrow[0:1, 0:256],
                                   rrow[0:1, 512:768], ALU.subtract)
        m_v = p.op("vector", f_stats, waits=[(sT, z_T)], inc=sV)
        p2m["statsV"] = m_v
        m_s = p.op("scalar", lambda sc: sc.activation(
            rrow[0:1, 512:768], rrow[0:1, 512:768], AF.Sqrt, bias=float(EPS)),
            waits=[(sV, m_v)], inc=sS)

        def f_r(v):
            v.reciprocal(rrow[0:1, 768:1024], rrow[0:1, 512:768])
            v.drain()
            v.tensor_tensor(rrow[0:1, 0:256], rrow[0:1, 256:512],
                            rrow[0:1, 768:1024], ALU.mult)
            v.drain()
            v.tensor_scalar(rrow[0:1, 0:256], rrow[0:1, 0:256],
                            -1.0, None, ALU.mult)
            return v.tensor_scalar(rrow[0:1, 768:1024], rrow[0:1, 768:1024],
                                   scol[0:1, 3:4], None, ALU.mult)
        m_r = p.op("vector", f_r, waits=[(sS, m_s)], inc=sV)

        m_tR = p.op("tensor", lambda t: t.matmul(
            pR[:, :], onesrow[0:1, :], rrow[0:1, 768:1024], start=True, stop=True),
            waits=[(sV, m_r)], inc=sT)
        m_tM = p.op("tensor", lambda t: t.matmul(
            pMu[:, :], onesrow[0:1, :], rrow[0:1, 0:256], start=True, stop=True),
            inc=sT)
        m_vR = p.op("vector", lambda v: v.tensor_copy(rrep[:, :], pR[:, :]),
                    waits=[(sT, m_tM)], inc=sV)
        m_vM = p.op("vector", lambda v: v.tensor_copy(murep[:, :], pMu[:, :]), inc=sV)

        poslot = (po, pMu)
        v1s = {}
        prev_v1 = 0
        for m in range(8):
            def f_w2(t, m=m):
                pom = poslot[m % 2]
                for k in range(32):
                    inst = t.matmul(pom[:, :],
                                    w2q[:, k * 1024 + m * 128: k * 1024 + (m + 1) * 128],
                                    h1g[:, k * 256:(k + 1) * 256],
                                    start=(k == 0), stop=(k == 31))
                return inst
            if m == 0:
                w = [(sS, prev_g)]
                if "poV" in p2m:
                    w.append((sV, p2m["poV"]))
            elif m == 1:
                w = [(sV, m_vM)]
            else:
                w = [(sV, v1s[m - 2])]
            m_w2 = p.op("tensor", f_w2, waits=w, inc=sT)

            def f_sum(v, m=m):
                v.tensor_tensor(sumt[:, :], poslot[m % 2][:, :], rrep[:, :], ALU.mult)
                v.drain()
                v.scalar_tensor_tensor(sumt[:, :], murep[:, :], vcol[:, m:m + 1],
                                       sumt[:, :], ALU.mult, ALU.add)
                v.drain()
                return v.scalar_tensor_tensor(sumt[:, :],
                                              cptw[:, cb + m * 256:cb + (m + 1) * 256],
                                              ubias[:, m:m + 1], sumt[:, :],
                                              ALU.add, ALU.add)
            prev_v1 = p.op("vector", f_sum, waits=[(sT, m_w2), (sG, m_lc)], inc=sV)
            v1s[m] = prev_v1
            m_st = prev_v1
            for tb in range(2):
                prev_tt = p.op("tensor", lambda t, tb=tb: t.transpose(
                    pft[:, 0:128], sumt[:, tb * 128:(tb + 1) * 128], identf[:, :]),
                    waits=[(sV, m_st)], inc=sT)
                ow = [(sT, prev_tt)]
                if m == 0 and tb == 0 and tile >= 1:
                    ow.append((sO, 1))
                m_st = p.op("vector", lambda v, tb=tb, m=m: v.scalar_tensor_tensor(
                    outtok[:, tb * 1024 + m * 128: tb * 1024 + (m + 1) * 128],
                    pft[:, 0:128], 1.0,
                    xtok2[:, xb + tb * 1024 + m * 128: xb + tb * 1024 + (m + 1) * 128],
                    ALU.mult, ALU.add), waits=ow, inc=sV)
        out_V = p.val(sV)
        p2m[("x", par3)] = out_V
        p2m["outV"] = out_V
        p2m["poV"] = prev_v1
        p2m["sTend"] = p.val(sT)

        p.op("gpsimd", lambda g: g.dma_start(
            bass.AP(dout, (seq * T + tok0) * C, [[C, 128], [128 * C, 2], [1, C]]),
            bass.AP(outtok, 0, [[2 * 1024, 128], [1024, 2], [1, 1024]])),
            waits=[(sV, out_V)], inc=sO, n=16)


def _host_layouts(inputs):
    f = np.float32
    W_sf = np.ascontiguousarray(np.asarray(inputs["L0_attn_W_sf"], f))
    w_bond = np.ascontiguousarray(np.asarray(inputs["L0_attn_w_bond"], f))
    w_dmg = np.ascontiguousarray(np.asarray(inputs["L0_attn_w_dmg"], f))
    sdw = np.zeros((2, 128, 128), f)
    for half in range(2):
        for hh in range(4):
            sdw[half, (half * 4 + hh) * 16:(half * 4 + hh) * 16 + 16,
                hh * 32:(hh + 1) * 32] = W_sf
    sdot = np.zeros((16, 128, 128), f)
    for jj in range(16):
        for hh in range(4):
            sdot[jj, hh * 32:hh * 32 + 16, hh * 16 + jj] = w_bond
            sdot[jj, hh * 32 + 16:hh * 32 + 32, 64 + hh * 16 + jj] = w_dmg
    wposx = np.zeros((16, 128), f)
    W_pos = np.asarray(inputs["L0_attn_W_pos"], f)
    for hh in range(4):
        wposx[:, hh * 32:hh * 32 + 16] = W_pos
    cvecm = np.zeros((128, 8), f)
    b_sf = np.asarray(inputs["L0_attn_b_sf"], f)
    for hh in range(4):
        cvecm[hh * 32:(hh + 1) * 32, 0] = b_sf
    cvecm[:, 1] = float(np.asarray(inputs["L0_attn_b_dmg"], f))
    relt = np.ascontiguousarray(np.asarray(inputs["L0_attn_rel_pos_emb"], f).T)

    def colchunks(vec, n):
        return np.ascontiguousarray(np.asarray(vec, f).reshape(n, 128).T)

    return dict(
        wdisp=np.ascontiguousarray(np.asarray(inputs["L0_attn_W_disp"], f)),
        wval=np.ascontiguousarray(np.asarray(inputs["L0_attn_W_val"], f)),
        wcproj=np.ascontiguousarray(np.asarray(inputs["L0_attn_W_cproj"], f)),
        w1t=np.ascontiguousarray(np.asarray(inputs["L0_ffn_w1"], f).T),
        w2t=np.ascontiguousarray(np.asarray(inputs["L0_ffn_w2"], f).T),
        sdw=sdw, sdot=sdot,
        ident=np.eye(128, dtype=f),
        cvec=cvecm, wposx=wposx, relt=relt,
        ln1g=colchunks(inputs["L0_ffn_ln1_g"], 8),
        ln1b=colchunks(inputs["L0_ffn_ln1_b"], 8),
        b1c=colchunks(inputs["L0_ffn_b1"], 32),
        g2c=colchunks(inputs["L0_ffn_ln2_g"], 32),
        b2lnc=colchunks(inputs["L0_ffn_ln2_b"], 32),
        b2fc=colchunks(inputs["L0_ffn_b2"], 8),
        onesrow=np.ones((1, 128), f),
        mask0=_mask0(),
        skm=_skm(),
    )


def _skm():
    m = np.zeros((128, 32), np.float32)
    for q in range(128):
        m[q, q // 4] = 1.0
    return m


def _mask0():
    m = np.ones((128, 1024), np.float32)
    for q in range(128):
        t = (q % 4) * 32 + q // 4
        if t < 31:
            for H in range(16):
                m[q, H * 64:H * 64 + (31 - t)] = 0.0
                m[q, H * 64 + 32:H * 64 + 32 + (31 - t)] = 0.0
    return m


def kernel(**inputs):
    nc = build_program()
    lay = _host_layouts(inputs)
    x = np.ascontiguousarray(np.asarray(inputs["L0_attn_x"], np.float32))
    in_maps = []
    for c in range(NCORES):
        m = dict(lay)
        m["x"] = np.ascontiguousarray(x[c * SPC:(c + 1) * SPC])
        in_maps.append(m)
    res = run_bass_kernel_spmd(nc, in_maps, core_ids=list(range(NCORES)))
    out = np.concatenate([r["out"] for r in res.results], axis=0)
    return out.astype(np.float32)


# version 06 (final)
# speedup vs baseline: 1.0341x; 1.0341x over previous
import numpy as np
import concourse.bass as bass
import concourse.mybir as mybir
from concourse.bass_utils import run_bass_kernel_spmd

F32 = mybir.dt.float32
BF16 = mybir.dt.bfloat16
AF = mybir.ActivationFunctionType
ALU = mybir.AluOpType
AX = mybir.AxisListType

B, T, C = 16, 2048, 1024
NCORES = 8
SPC = B // NCORES
NH, HS, BDIM, DELTA = 16, 64, 16, 32
HID = 4096
EPS = 1e-5
NBLK = T // 128
TW2 = 256
ENGINES = ("tensor", "vector", "scalar", "gpsimd", "sync")
VTS = 17 * 1040


class Prog:

    def __init__(self):
        self.ops = {e: [] for e in ENGINES}
        self.v = {}
        self.dma_sems = set()
        self.ckpts = []

    def ckpt(self, name):
        self.ckpts.append((name, {e: len(self.ops[e]) for e in ENGINES},
                           dict(self.v)))

    def cut(self, name, sems):
        for nm, lens, vals in self.ckpts:
            if nm == name:
                for e in ENGINES:
                    self.ops[e] = self.ops[e][:lens[e]]
                snap = [(s, v) for s, v in vals.items() if v > 0]
                for e in ENGINES:
                    self.ops[e].append((snap, None, None, 0))
                return
        raise KeyError(name)

    def val(self, sem):
        return self.v.get(sem, 0)

    def op(self, eng, fn, waits=(), inc=None, n=1):
        waits = [(sem, self.val(sem) if sem in self.dma_sems else v)
                 for (sem, v) in waits]
        self.ops[eng].append((list(waits), fn, inc, n))
        if inc is not None:
            self.v[inc] = self.v.get(inc, 0) + n
            return self.v[inc]
        return None

    def wait(self, eng, waits):
        self.ops[eng].append((list(waits), None, None, 0))

    def barrier(self, sems):
        snap = [(s, self.val(s)) for s in sems if self.val(s) > 0]
        for e in ENGINES:
            self.ops[e].append((snap, None, None, 0))

    def replay(self, eng_obj, name):
        autodrain = name in ("vector", "scalar", "gpsimd")
        for waits, fn, inc, n in self.ops[name]:
            for sem, val in waits:
                if val > 0:
                    eng_obj.wait_ge(sem, val)
            if fn is not None:
                if autodrain:
                    eng_obj.drain()
                inst = fn(eng_obj)
                if inc is not None:
                    inst.then_inc(inc, n)


def build_program(cut=None):
    nc = bass.Bass("TRN2", target_bir_lowering=False)
    p = Prog()

    dx = nc.dram_tensor("x", [SPC, T, C], F32, kind="ExternalInput")
    dout = nc.dram_tensor("out", [SPC, T, C], F32, kind="ExternalOutput")
    d_wdisp = nc.dram_tensor("wdisp", [C, 256], F32, kind="ExternalInput")
    d_wval = nc.dram_tensor("wval", [C, C], F32, kind="ExternalInput")
    d_wcproj = nc.dram_tensor("wcproj", [C, C], F32, kind="ExternalInput")
    d_w1t = nc.dram_tensor("w1t", [C, HID], F32, kind="ExternalInput")
    d_w2t = nc.dram_tensor("w2t", [HID, C], F32, kind="ExternalInput")
    d_sdw = nc.dram_tensor("sdw", [2, 128, 128], F32, kind="ExternalInput")
    d_sdot = nc.dram_tensor("sdot", [16, 128, 128], F32, kind="ExternalInput")
    d_ident = nc.dram_tensor("ident", [128, 128], F32, kind="ExternalInput")
    d_cvec = nc.dram_tensor("cvec", [128, 8], F32, kind="ExternalInput")
    d_wposx = nc.dram_tensor("wposx", [16, 128], F32, kind="ExternalInput")
    d_relt = nc.dram_tensor("relt", [16, 32], F32, kind="ExternalInput")
    d_ln1g = nc.dram_tensor("ln1g", [128, 8], F32, kind="ExternalInput")
    d_ln1b = nc.dram_tensor("ln1b", [128, 8], F32, kind="ExternalInput")
    d_b1 = nc.dram_tensor("b1c", [128, 32], F32, kind="ExternalInput")
    d_g2 = nc.dram_tensor("g2c", [128, 32], F32, kind="ExternalInput")
    d_b2ln = nc.dram_tensor("b2lnc", [128, 32], F32, kind="ExternalInput")
    d_b2f = nc.dram_tensor("b2fc", [128, 8], F32, kind="ExternalInput")
    d_ones = nc.dram_tensor("onesrow", [1, 128], F32, kind="ExternalInput")
    d_mask0 = nc.dram_tensor("mask0", [128, 1024], F32, kind="ExternalInput")
    d_skm = nc.dram_tensor("skm", [128, 32], F32, kind="ExternalInput")
    d_cp = nc.dram_tensor("cpscratch", [SPC, 8, 128, T], BF16)

    from contextlib import ExitStack
    with ExitStack() as _st:
        def SB(name, shape, dt):
            return _st.enter_context(nc.sbuf_tensor(name, shape, dt))
        identb = SB("identb", [128, 128], BF16)
        identf = SB("identf", [128, 128], F32)
        cbias = SB("cbias", [128, 32], F32)
        cvec = SB("s_cvec", [128, 8], F32)
        onescb = SB("onescb", [128, 1], BF16)
        onescf = SB("onescf", [128, 1], F32)
        onesrow = SB("s_onesrow", [1, 128], F32)
        ln1gb = SB("ln1gb", [128, 8], F32)
        ln1bb = SB("ln1bb", [128, 8], F32)
        b1c = SB("s_b1c", [128, 32], F32)
        g2c = SB("s_g2c", [128, 32], F32)
        b2lnc = SB("s_b2lnc", [128, 32], F32)
        b2fc = SB("s_b2fc", [128, 8], F32)
        stat = SB("stat", [128, 16], F32)
        constt = SB("constt", [128, 4], F32)
        nc.const_aps.register(F32, 0.0, constt[:, 0:1])
        nc.const_aps.register(F32, 1.0, constt[:, 1:2])
        nc.const_aps.register(F32, float(EPS), constt[:, 2:3])
        sD = _st.enter_context(nc.semaphore("sD"))
        sG = _st.enter_context(nc.semaphore("sG"))
        sT = _st.enter_context(nc.semaphore("sT"))
        sV = _st.enter_context(nc.semaphore("sV"))
        sS = _st.enter_context(nc.semaphore("sS"))
        sP = _st.enter_context(nc.semaphore("sP"))
        sO = _st.enter_context(nc.semaphore("sO"))
        sX = _st.enter_context(nc.semaphore("sX"))
        sQa = _st.enter_context(nc.semaphore("sQa"))
        sQb = _st.enter_context(nc.semaphore("sQb"))
        True
        SEMS = (sD, sG, sT, sV, sS, sP, sO, sX, sQa, sQb)
        p.dma_sems = {sD, sG, sO, sX, sQa, sQb}
        L = dict(locals())

        for dst, src in [(identf, d_ident), (cvec, d_cvec), (ln1gb, d_ln1g),
                         (ln1bb, d_ln1b), (b1c, d_b1), (g2c, d_g2),
                         (b2lnc, d_b2ln), (b2fc, d_b2f), (onesrow, d_ones)]:
            p.op("sync", lambda s, dst=dst, src=src: s.dma_start(dst[:, :], src[:, :]),
                 inc=sD, n=16)
        p.op("gpsimd", lambda g: g.dma_start(identb[:, :], d_ident[:, :]), inc=sG, n=16)
        consts_D = p.val(sD)
        p.ckpt("c1")
        p.op("vector", lambda v: v.memset(constt[:, 0:1], 0.0), inc=sV)
        p.op("vector", lambda v: v.memset(constt[:, 1:2], 1.0), inc=sV)
        p.op("vector", lambda v: v.memset(constt[:, 2:3], float(EPS)), inc=sV)
        p.op("vector", lambda v: v.memset(onescb[:, :], 1.0), inc=sV)
        p.op("vector", lambda v: v.memset(onescf[:, :], 1.0), inc=sV)
        p.ckpt("c2")

        with nc.sbuf_tensor("s_wposx", [16, 128], F32) as wposx, \
             nc.sbuf_tensor("s_relt", [16, 32], F32) as relt, \
             nc.psum_tensor("pc0", [128, 32], F32) as pc0:
            p.op("sync", lambda s: s.dma_start(wposx[:, :], d_wposx[:, :]), inc=sD, n=16)
            p.op("sync", lambda s: s.dma_start(relt[:, :], d_relt[:, :]), inc=sD, n=16)
            pw_D = p.val(sD)
            p.ckpt("c3")
            t_pf = p.op("tensor",
                        lambda t: t.matmul(pc0[:, :], wposx[:, :], relt[:, :],
                                           start=True, stop=True),
                        waits=[(sD, pw_D)], inc=sT)
            p.op("vector",
                 lambda v: v.tensor_scalar(cbias[:, :], pc0[:, :], cvec[:, 0:1],
                                           None, ALU.add),
                 waits=[(sT, t_pf), (sD, consts_D)], inc=sV)
        L["cbias_V"] = p.val(sV)
        p.barrier(SEMS)
        p.ckpt("ph0")

        with ExitStack() as _s1:
            def SB1(name, shape, dt):
                return _s1.enter_context(nc.sbuf_tensor(name, shape, dt))
            wdisp = SB1("s_wdisp", [128, 8 * 256], BF16)
            wval = SB1("s_wval", [128, 8 * 1024], BF16)
            wcproj = SB1("s_wcproj", [128, 8 * 1024], BF16)
            sdw = SB1("s_sdw", [128, 2 * 128], BF16)
            sdot = SB1("s_sdot", [128, 16 * 128], BF16)
            xntok = SB1("xntok", [128, 1024], BF16)
            xnfm = SB1("xnfm", [128, 8 * 2048], BF16)
            dwseq = SB1("dwseq", [128, 4 * 2080], BF16)
            dispfm = SB1("dispfm", [128, 2 * 512], BF16)
            valfm = SB1("valfm", [128, 8 * 512], BF16)
            valtok = SB1("valtok", [128, VTS], BF16)
            attntok = SB1("attntok", [128, 4 * 1024], BF16)
            ebuf = SB1("ebuf", [128, 2 * 512], BF16)
            gbuf = SB1("gbuf", [128, 2 * 512], BF16)
            sigb = SB1("sigb", [128, 2 * 512], F32)
            bondb = SB1("bondb", [128, 2 * 512], F32)
            stok = SB1("stok", [128, 64], F32)
            stok2 = SB1("stok2", [128, 64], F32)
            expw = SB1("expw", [128, 4 * 1024], BF16)
            rot = SB1("rot", [128, 512], BF16)
            tdd = SB1("tdd", [128, 512], BF16)
            tee = SB1("tee", [128, 512], BF16)
            dd = SB1("dd", [128, 512], BF16)
            ee = SB1("ee", [128, 512], BF16)
            rz = SB1("rz", [128, 16], F32)
            sumA = SB1("sumA", [128, 65], F32)
            sumB = SB1("sumB", [128, 65], F32)
            attnfm = SB1("attnfm", [128, 8 * 512], BF16)
            cptile = SB1("cptile", [128, 512], BF16)
            mask0 = SB1("s_mask0", [128, 1024], BF16)
            skm = SB1("s_skm", [128, 32], F32)
            xtok = SB1("xtok", [128, 1024], F32)
            xnf32 = SB1("xnf32", [128, 1024], F32)
            L["xtok"] = xtok
            L["xnf32"] = xnf32
            True
            L.update(dict(wdisp=wdisp, wval=wval, wcproj=wcproj, sdw=sdw,
                          sdot=sdot, xntok=xntok, xnfm=xnfm, dwseq=dwseq,
                          dispfm=dispfm, valfm=valfm, valtok=valtok,
                          attntok=attntok, ebuf=ebuf, gbuf=gbuf, sigb=sigb,
                          bondb=bondb, stok=stok, stok2=stok2, expw=expw, rot=rot, tdd=tdd,
                          tee=tee, dd=dd, ee=ee, rz=rz, sumA=sumA, sumB=sumB, attnfm=attnfm,
                          cptile=cptile))
            p.op("gpsimd", lambda g: g.dma_start(
                bass.AP(wdisp, 0, [[8 * 256, 128], [256, 8], [1, 256]]),
                bass.AP(d_wdisp, 0, [[256, 128], [128 * 256, 8], [1, 256]])), inc=sG, n=16)
            p.op("gpsimd", lambda g: g.dma_start(
                bass.AP(wval, 0, [[8 * 1024, 128], [1024, 8], [1, 1024]]),
                bass.AP(d_wval, 0, [[1024, 128], [128 * 1024, 8], [1, 1024]])), inc=sG, n=16)
            p.op("gpsimd", lambda g: g.dma_start(
                bass.AP(wcproj, 0, [[8 * 1024, 128], [1024, 8], [1, 1024]]),
                bass.AP(d_wcproj, 0, [[1024, 128], [128 * 1024, 8], [1, 1024]])), inc=sG, n=16)
            p.op("gpsimd", lambda g: g.dma_start(
                bass.AP(sdw, 0, [[2 * 128, 128], [128, 2], [1, 128]]),
                bass.AP(d_sdw, 0, [[128, 128], [128 * 128, 2], [1, 128]])), inc=sG, n=16)
            p.op("gpsimd", lambda g: g.dma_start(
                bass.AP(sdot, 0, [[16 * 128, 128], [128, 16], [1, 128]]),
                bass.AP(d_sdot, 0, [[128, 128], [128 * 128, 16], [1, 128]])), inc=sG, n=16)
            p.op("gpsimd", lambda g: g.dma_start(mask0[:, :], d_mask0[:, :]), inc=sG, n=16)
            p.op("sync", lambda s_: s_.dma_start(skm[:, :], d_skm[:, :]), inc=sD, n=16)
            L["wts_G"] = p.val(sG)
            L["mask0"] = mask0
            L["skm"] = skm

            for seq in range(SPC):
                phase1_seq(nc, p, seq, L)
                p.ckpt(f"p1seq{seq}")

        p.barrier(SEMS)

        with ExitStack() as _s2:
            def SB2(name, shape, dt):
                return _s2.enter_context(nc.sbuf_tensor(name, shape, dt))
            w1q = SB2("w1q", [128, 8 * 4096], BF16)
            w2q = SB2("w2q", [128, 32 * 1024], BF16)
            scol = SB2("scol", [128, 8], F32)
            qrow = SB2("qrow", [1, 8], F32)
            ucol = SB2("ucol", [128, 8], F32)
            vcol = SB2("vcol", [128, 8], F32)
            ubias = SB2("ubias", [128, 8], F32)
            b2lnb = SB2("b2lnb", [128, 32], BF16)
            True
            L.update(dict(w1q=w1q, w2q=w2q, scol=scol, qrow=qrow, ucol=ucol,
                          vcol=vcol, ubias=ubias, b2lnb=b2lnb))
            quantize_weights(nc, p, L)
            p.barrier(SEMS)
            p.ckpt("quant")

            with ExitStack() as _s3:
                def SB3(name, shape, dt):
                    return _s3.enter_context(nc.sbuf_tensor(name, shape, dt))
                xtok2 = SB3("xtok2", [128, 3 * 2048], F32)
                xn2 = SB3("xn2", [128, 1024], F32)
                ln1tok = SB3("ln1tok", [128, 1024], BF16)
                ln1fm = SB3("ln1fm", [128, 2 * 8 * 256], BF16)
                h1g = SB3("h1g", [128, 32 * 256], BF16)
                sqt = SB3("sqt", [128, 256], BF16)
                rrow = SB3("rrow", [1, 4 * 256], F32)
                rrep = SB3("rrep", [128, 256], F32)
                murep = SB3("murep", [128, 256], F32)
                cptw = SB3("cptw", [128, 2 * 8 * 256], BF16)
                sumt = SB3("sumt", [128, 256], F32)
                outtok = SB3("outtok", [128, 2 * 1024], F32)
                psc = _s3.enter_context(nc.psum_tensor("psc", [128, 512], F32))
                True
                L.update(dict(xtok2=xtok2, xn2=xn2, ln1tok=ln1tok,
                              ln1fm=ln1fm, h1g=h1g, sqt=sqt, rrow=rrow,
                              rrep=rrep, murep=murep, cptw=cptw,
                              sumt=sumt, outtok=outtok, psc=psc))
                ntiles = SPC * T // TW2
                phase2_front(nc, p, 0, L)
                phase2_front(nc, p, 1, L)
                for tile in range(ntiles):
                    phase2_back(nc, p, tile, L)
                    if tile == 0:
                        p.ckpt("tile0")
                    if tile + 2 < ntiles:
                        phase2_front(nc, p, tile + 2, L)

        if cut is not None:
            p.cut(cut, SEMS)
        else:
            p.op("sync", lambda s: s.nop(),
                 waits=[(sD, p.val(sD)), (sG, p.val(sG)), (sO, p.val(sO)),
                        (sX, p.val(sX))])

        with nc.Block() as block:
            @block.tensor
            def _(e):
                p.replay(e, "tensor")

            @block.vector
            def _(e):
                p.replay(e, "vector")

            @block.scalar
            def _(e):
                p.replay(e, "scalar")

            @block.gpsimd
            def _(e):
                p.replay(e, "gpsimd")

            @block.sync
            def _(e):
                p.replay(e, "sync")

    return nc


def ln_stats_apply(nc, p, L, src, srcslice, out, outslice, waitin, scratch=None):
    sV, sS = L["sV"], L["sS"]
    stat = L["stat"]
    s0 = srcslice[0]
    m_v1 = p.op("vector", lambda v: v.tensor_reduce(
        stat[:, 1:2], src[:, s0:s0 + 1024], AX.X, ALU.add),
        waits=waitin, inc=sV)
    psc = scratch if scratch is not None else L["psc"]
    p.op("scalar", lambda sc: sc.activation(
        psc[:, 0:512], src[:, s0:s0 + 512], AF.Square,
        accum_out=stat[:, 11:12]), waits=waitin, inc=sS)
    m_s1 = p.op("scalar", lambda sc: sc.activation(
        psc[:, 0:512], src[:, s0 + 512:s0 + 1024], AF.Square,
        accum_out=stat[:, 12:13]), inc=sS)

    def f_v2(v):
        v.tensor_tensor(stat[:, 2:3], stat[:, 11:12], stat[:, 12:13], ALU.add)
        v.drain()
        v.tensor_scalar(stat[:, 3:4], stat[:, 1:2], 1.0 / C, None, ALU.mult)
        v.tensor_scalar(stat[:, 4:5], stat[:, 2:3], 1.0 / C, None, ALU.mult)
        v.drain()
        v.tensor_tensor(stat[:, 5:6], stat[:, 3:4], stat[:, 3:4], ALU.mult)
        v.drain()
        return v.tensor_tensor(stat[:, 6:7], stat[:, 4:5], stat[:, 5:6], ALU.subtract)
    m_v2 = p.op("vector", f_v2, waits=[(sV, m_v1), (sS, m_s1)], inc=sV)
    m_s2 = p.op("scalar", lambda sc: sc.activation(
        stat[:, 7:8], stat[:, 6:7], AF.Sqrt, bias=float(EPS)),
        waits=[(sV, m_v2)], inc=sS)

    def f_v3(v):
        v.reciprocal(stat[:, 8:9], stat[:, 7:8])
        v.drain()
        v.tensor_tensor(stat[:, 9:10], stat[:, 3:4], stat[:, 8:9], ALU.mult)
        v.drain()
        v.tensor_scalar(stat[:, 10:11], stat[:, 9:10], -1.0, None, ALU.mult)
        v.drain()
        return v.tensor_scalar(out[:, outslice[0]:outslice[1]],
                               src[:, s0:s0 + 1024],
                               stat[:, 8:9], stat[:, 10:11], ALU.mult, ALU.add)
    return p.op("vector", f_v3, waits=[(sS, m_s2)], inc=sV)


def phase1_seq(nc, p, seq, L):
    sD, sG, sT, sV, sS, sP = L["sD"], L["sG"], L["sT"], L["sV"], L["sS"], L["sP"]
    dx, d_cp = L["dx"], L["d_cp"]
    identb, identf, cvec = L["identb"], L["identf"], L["cvec"]
    stat, xtok, xnf32 = L["stat"], L["xtok"], L["xnf32"]
    wdisp, wval, wcproj, sdw = L["wdisp"], L["wval"], L["wcproj"], L["sdw"]
    xntok, xnfm, dwseq = L["xntok"], L["xnfm"], L["dwseq"]
    dispfm, valfm, valtok, attntok = L["dispfm"], L["valfm"], L["valtok"], L["attntok"]
    attnfm, cptile = L["attnfm"], L["cptile"]
    wts_G = L["wts_G"]

    prev_tr = 0
    for blk in range(NBLK):
        waits = [(sD, prev_tr)] if prev_tr else []
        m_load = p.op("sync", lambda s, seq=seq, blk=blk: s.dma_start(
            xtok[:, :], bass.AP(dx, (seq * T + blk * 128) * C, [[C, 128], [1, C]])),
            waits=waits, inc=sD, n=16)
        m_xn = ln_stats_apply(nc, p, L, xtok, (0, 1024), xnf32, (0, 1024),
                              [(sD, m_load)], scratch=xnf32)
        m_cp = p.op("vector", lambda v: v.tensor_copy(xntok[:, :], xnf32[:, :]),
                    waits=([(sD, prev_tr)] if prev_tr else []), inc=sV)
        for cc in range(8):
            p.op("sync", lambda s, cc=cc, blk=blk: s.dma_start(
                bass.AP(xnfm, cc * 2048 + blk * 128, [[8 * 2048, 128], [1, 128]]),
                bass.AP(xntok, cc * 128, [[1024, 128], [1, 128]]),
                transpose=True),
                waits=([(sV, m_cp)] if cc == 0 else []), inc=sD, n=16)
        prev_tr = p.val(sD)
    xnfm_D = p.val(sD)
    p.ckpt(f"p1a_{seq}")

    def f_init(v):
        for g4 in range(4):
            v.memset(dwseq[:, g4 * 2080:g4 * 2080 + 32], 0.0)
        v.memset(valtok[:, 0:1040], 0.0)
        v.drain()
        return v.memset(bass.AP(valtok, 64, [[VTS, 128], [1040, 17], [65, 16]]), 1.0)
    init_V = p.op("vector", f_init, waits=[(sT, p.val(sT))], inc=sV)

    for tw in range(4):
        with nc.psum_tensor(f"pp_{seq}_{tw}", [128, 512], F32) as pp, \
             nc.psum_tensor(f"pvt_{seq}_{tw}", [128, 2048], BF16) as pvt:
            prev_evac = 0
            for m in range(2):
                def f_disp(t, m=m, tw=tw):
                    for k in range(8):
                        inst = t.matmul(
                            pp[:, :],
                            bass.AP(wdisp, k * 256 + m * 128, [[8 * 256, 128], [1, 128]]),
                            bass.AP(xnfm, k * 2048 + tw * 512, [[8 * 2048, 128], [1, 512]]),
                            start=(k == 0), stop=(k == 7))
                    return inst
                w = []
                if tw == 0 and m == 0:
                    w = [(sD, xnfm_D), (sG, wts_G)]
                if prev_evac:
                    w.append((sS, prev_evac))
                m_t = p.op("tensor", f_disp, waits=w, inc=sT)
                prev_evac = p.op("scalar", lambda sc, m=m: sc.copy(
                    dispfm[:, m * 512:(m + 1) * 512], pp[:, :]),
                    waits=[(sT, m_t)], inc=sS)
            disp_S = prev_evac

            for g4 in range(4):
                dgrp, half = g4 // 2, g4 % 2
                m_t = p.op("tensor", lambda t, dgrp=dgrp, half=half: t.matmul(
                    pp[:, :], sdw[:, half * 128:(half + 1) * 128],
                    dispfm[:, dgrp * 512:(dgrp + 1) * 512],
                    start=True, stop=True),
                    waits=[(sS, prev_evac)], inc=sT)
                prev_evac = p.op("scalar", lambda sc, g4=g4, tw=tw: sc.copy(
                    dwseq[:, g4 * 2080 + 32 + tw * 512: g4 * 2080 + 32 + (tw + 1) * 512],
                    pp[:, :]), waits=[(sT, m_t)], inc=sS)

            for m in range(8):
                def f_val(t, m=m, tw=tw):
                    for k in range(8):
                        inst = t.matmul(
                            pp[:, :],
                            bass.AP(wval, k * 1024 + m * 128, [[8 * 1024, 128], [1, 128]]),
                            bass.AP(xnfm, k * 2048 + tw * 512, [[8 * 2048, 128], [1, 512]]),
                            start=(k == 0), stop=(k == 7))
                    return inst
                m_t = p.op("tensor", f_val, waits=[(sS, prev_evac)], inc=sT)
                prev_evac = p.op("scalar", lambda sc, m=m: sc.copy(
                    valfm[:, m * 512:(m + 1) * 512], pp[:, :]),
                    waits=[(sT, m_t)], inc=sS)
            val_S = prev_evac

            prev_vte = 0
            for blkw in range(4):
                blk = tw * 4 + blkw
                vmarks = {}
                for vg in range(8):
                    w = []
                    if vg == 0:
                        w = [(sS, val_S), (sV, init_V)]
                        if prev_vte:
                            w.append((sV, prev_vte))
                    if vg >= 2:
                        w.append((sT, vmarks[vg - 2]))
                    slot = (vg % 2) * 1024 + (vg // 2) * 128
                    vmarks[vg] = p.op("tensor", lambda t, vg=vg, blkw=blkw, slot=slot: t.transpose(
                        pvt[:, slot:slot + 128],
                        bass.AP(valfm, vg * 512 + blkw * 128, [[8 * 512, 128], [1, 128]]),
                        identb[:, :]), waits=w, inc=sT)
                vt_T = p.val(sT)
                for vgpar in range(2):
                    prev_vte = p.op("vector", lambda v, blk=blk, vgpar=vgpar: v.tensor_copy(
                        bass.AP(valtok, (blk + 1) * 1040 + vgpar * 130,
                                [[VTS, 128], [260, 4], [65, 2], [1, 64]]),
                        bass.AP(pvt, vgpar * 1024, [[2048, 128], [128, 4], [64, 2], [1, 64]])),
                        waits=[(sT, vt_T)] if vgpar == 0 else [], inc=sV)
            vt_V = prev_vte

        p.ckpt(f"p1b_{seq}_{tw}")
        attn_core_tw(nc, p, seq, tw, L, vt_V)
        p.ckpt(f"p1c_{seq}_{tw}")

        with nc.psum_tensor(f"pcp_{seq}_{tw}", [128, 128], BF16) as pcp, \
             nc.psum_tensor(f"pcj_{seq}_{tw}", [128, 512], F32) as pcj:
            prev_ev = 0
            for cc in range(8):
                for blkw in range(4):
                    w = [(sV, prev_ev)] if prev_ev else []
                    if cc == 0 and blkw == 0:
                        w.append((sS, p.val(sS)))
                    m_t = p.op("tensor", lambda t, cc=cc, blkw=blkw: t.transpose(
                        pcp[:, :],
                        bass.AP(attntok, blkw * 1024 + cc * 128, [[4 * 1024, 128], [1, 128]]),
                        identb[:, :]), waits=w, inc=sT)
                    prev_ev = p.op("vector", lambda v, cc=cc, blkw=blkw: v.tensor_copy(
                        attnfm[:, cc * 512 + blkw * 128: cc * 512 + (blkw + 1) * 128],
                        pcp[:, :]), waits=[(sT, m_t)], inc=sV)
            afm_V = prev_ev
            prev_dma = 0
            for m in range(8):
                def f_cp(t, m=m):
                    for k in range(8):
                        inst = t.matmul(
                            pcj[:, :],
                            bass.AP(wcproj, k * 1024 + m * 128, [[8 * 1024, 128], [1, 128]]),
                            bass.AP(attnfm, k * 512, [[8 * 512, 128], [1, 512]]),
                            start=(k == 0), stop=(k == 7))
                    return inst
                w = [(sV, afm_V)] if m == 0 else []
                if prev_dma:
                    w.append((sS, prev_me))
                m_t = p.op("tensor", f_cp, waits=w, inc=sT)
                ew = [(sT, m_t), (sD, p.val(sD))]
                prev_me = p.op("scalar", lambda sc: sc.copy(cptile[:, :], pcj[:, :]),
                               waits=ew, inc=sS)
                prev_dma = p.op("sync", lambda s, seq=seq, m=m, tw=tw: s.dma_start(
                    bass.AP(d_cp, ((seq * 8 + m) * 128) * T + tw * 512, [[T, 128], [1, 512]]),
                    cptile[:, :]), waits=[(sS, prev_me)], inc=sD, n=16)


def attn_core_tw(nc, p, seq, tw, L, vt_V):
    sD, sG, sT, sV, sS, sP = L["sD"], L["sG"], L["sT"], L["sV"], L["sS"], L["sP"]
    identb, identf, cbias, cvec = L["identb"], L["identf"], L["cbias"], L["cvec"]
    sdot, dwseq = L["sdot"], L["dwseq"]
    valtok, attntok = L["valtok"], L["attntok"]
    ebuf, gbuf, sigb, bondb, stok = L["ebuf"], L["gbuf"], L["sigb"], L["bondb"], L["stok"]
    stok2 = L["stok2"]
    expw, rot, tdd, tee, dd, ee, rz = (L["expw"], L["rot"], L["tdd"], L["tee"],
                                       L["dd"], L["ee"], L["rz"])
    sumA, sumB = L["sumA"], L["sumB"]
    cbias_V = L["cbias_V"]

    with nc.psum_tensor(f"pjlo_{seq}_{tw}", [128, 512], F32) as pjlo, \
         nc.psum_tensor(f"pjhi_{seq}_{tw}", [128, 512], F32) as pjhi, \
         nc.psum_tensor(f"psand_{seq}_{tw}", [128, 1024], F32) as psand:
        exp_marks = []
        for g in range(4):
            dwoff = g * 2080 + 32 + tw * 512
            g_marks = [0, 0]
            mm_marks = [0, 0]
            for j in range(32):
                par = j % 2
                jh, jj = j // 16, j % 16
                pj = pjlo if jh == 0 else pjhi
                waits = []
                if j == 0 and g == 0:
                    waits += [(sS, p.val(sS))]
                    if tw == 0 and seq == 0:
                        waits += [(sV, cbias_V)]
                if g_marks[par]:
                    waits.append((sS, g_marks[par]))
                m_e = p.op("vector", lambda v, dwoff=dwoff, j=j, par=par: v.tensor_tensor(
                    ebuf[:, par * 512:(par + 1) * 512],
                    bass.AP(dwseq, dwoff + (j - 31), [[4 * 2080, 128], [1, 512]]),
                    bass.AP(dwseq, dwoff, [[4 * 2080, 128], [1, 512]]),
                    ALU.subtract), waits=waits, inc=sV)
                gw = [(sV, m_e)]
                if mm_marks[par]:
                    gw.append((sT, mm_marks[par]))
                m_g = p.op("scalar", lambda sc, j=j, par=par: sc.activation(
                    gbuf[:, par * 512:(par + 1) * 512],
                    ebuf[:, par * 512:(par + 1) * 512],
                    AF.Gelu, bias=cbias[:, j:j + 1]), waits=gw, inc=sS)
                g_marks[par] = m_g
                m_t = p.op("tensor", lambda t, jj=jj, par=par, pj=pj: t.matmul(
                    pj[:, :], sdot[:, jj * 128:(jj + 1) * 128],
                    gbuf[:, par * 512:(par + 1) * 512],
                    start=(jj == 0), stop=(jj == 15)),
                    waits=[(sS, m_g)], inc=sT)
                mm_marks[par] = m_t
            dots_T = p.val(sT)
            if g == 0:
                p.ckpt(f"jloop_{seq}_{tw}")

            for jh in range(2):
                pj = pjlo if jh == 0 else pjhi
                p.op("scalar", lambda sc, pj=pj, jh=jh: sc.activation(
                    bass.AP(sigb, 64 * 1024 + jh * 512,
                            [[1024, 64], [128, 4], [1, 4], [4, 32]]),
                    pj[64:128, :], AF.Sigmoid, bias=cvec[64:128, 1:2]),
                    waits=[(sT, dots_T)], inc=sS)
                p.op("vector", lambda v, pj=pj, jh=jh: v.tensor_copy(
                    bass.AP(bondb, jh * 512, [[1024, 64], [128, 4], [1, 4], [4, 32]]),
                    pj[0:64, :]), waits=[(sT, dots_T)], inc=sV)
            sig_S, bond_V = p.val(sS), p.val(sV)
            if g == 0:
                p.ckpt(f"sig_{seq}_{tw}")

            for jh in range(2):
                for blkw in range(4):
                    m_ta = p.op("tensor", lambda t, jh=jh, blkw=blkw: t.transpose(
                        psand[:, 0:64],
                        bass.AP(bondb, jh * 512 + blkw * 128, [[1024, 64], [1, 128]]),
                        identf[0:64, 0:64]),
                        waits=[(sS, sig_S), (sV, p.val(sV))], inc=sT)
                    if g == 0 and jh == 0 and blkw == 0:
                        p.ckpt(f"tA_{seq}_{tw}")
                    m_tb = p.op("tensor", lambda t, jh=jh, blkw=blkw: t.transpose(
                        psand[:, 512:576],
                        bass.AP(sigb, 64 * 1024 + jh * 512 + blkw * 128, [[1024, 64], [1, 128]]),
                        identf[64:128, 64:128]), inc=sT)
                    if g == 0 and jh == 0 and blkw == 0:
                        p.ckpt(f"tB_{seq}_{tw}")
                    m_cb0 = p.op("scalar", lambda sc: sc.copy(
                        stok2[:, :], psand[:, 0:64]),
                        waits=[(sT, m_tb), (sV, p.val(sV))], inc=sS)
                    m_cb = p.op("vector", lambda v: v.scalar_tensor_tensor(
                        stok[:, :], psand[:, 512:576], -10.0, stok2[:, :],
                        ALU.mult, ALU.add),
                        waits=[(sT, m_tb), (sS, m_cb0)], inc=sV)
                    if g == 0 and jh == 0 and blkw == 0:
                        p.ckpt(f"comb_{seq}_{tw}")
                    m_ex = p.op("scalar", lambda sc, g=g, jh=jh, blkw=blkw: sc.activation(
                        bass.AP(expw, blkw * 1024 + g * 256 + jh * 16,
                                [[4 * 1024, 128], [64, 4], [1, 16]]),
                        stok[:, :], AF.Exp), waits=[(sV, m_cb)], inc=sS)
                    exp_marks.append(m_ex)
                    if g == 0 and jh == 0 and blkw == 0:
                        p.ckpt(f"sand1_{seq}_{tw}")
        exp_S = p.val(sS)
        p.ckpt(f"dots_{seq}_{tw}")
        mask0 = L["mask0"]
        for blkw in range(4):
            blk = tw * 4 + blkw
            base = blkw * 1024
            m_dup = p.op("vector", lambda v, base=base: v.tensor_copy(
                bass.AP(expw, base + 32, [[4096, 128], [64, 16], [1, 32]]),
                bass.AP(expw, base, [[4096, 128], [64, 16], [1, 32]])),
                waits=[(sS, exp_S)], inc=sV)
            if blk == 0:
                m_dup = p.op("vector", lambda v, base=base: v.tensor_tensor(
                    expw[:, base:base + 1024], expw[:, base:base + 1024],
                    mask0[:, :], ALU.mult), waits=[(sV, m_dup)], inc=sV)

            skm = L["skm"]
            m_z = p.op("vector", lambda v: v.memset(rot[:, :], 0.0),
                       waits=[(sV, m_dup), (sT, p.val(sT))], inc=sV)
            for s_ in range(32):
                p.op("vector", lambda v, base=base, s_=s_: v.scalar_tensor_tensor(
                    rot[:, :],
                    bass.AP(expw, base + (31 - s_), [[4096, 128], [64, 16], [1, 32]]),
                    skm[:, s_:s_ + 1], rot[:, :], ALU.mult, ALU.add),
                    waits=[(sV, m_z)] if s_ == 0 else [], inc=sV)
            skew_V = p.val(sV)
            if blkw == 0:
                p.ckpt(f"skew_{seq}_{tw}")

            with nc.psum_tensor(f"prot_{seq}_{tw}_{blkw}", [128, 2048], BF16) as prot, \
                 nc.psum_tensor(f"pout_{seq}_{tw}_{blkw}", [128, 1024], F32) as pout:
                hmarks = {}
                for W in range(4):
                    for h in range(16):
                        w = []
                        if W == 0 and h == 0:
                            w = [(sV, skew_V), (sP, p.val(sP))]
                        if h >= 2:
                            w.append((sT, hmarks[(W, h - 2)]))
                        elif W > 0:
                            w.append((sT, hmarks[(W - 1, 14 + h)]))
                        slot = (h % 2) * 1024 + (h // 2) * 128
                        hmarks[(W, h)] = p.op("tensor", lambda t, W=W, h=h, slot=slot: t.transpose(
                            prot[32 * W:32 * W + 32, slot:slot + 128],
                            rot[:, h * 32:(h + 1) * 32], identb[:, :],
                            tile_position=(0, 32 * W)), waits=w, inc=sT)
                    rotT_T = p.val(sT)
                    m_ed = p.op("vector", lambda v, W=W: v.tensor_copy(
                        bass.AP(tdd, (32 * W) * 512, [[512, 32], [32, 2], [64, 8], [1, 32]]),
                        bass.AP(prot, (32 * W) * 2048 + W, [[2048, 32], [1024, 2], [128, 8], [4, 32]])),
                        waits=[(sT, rotT_T), (sP, p.val(sP))], inc=sV)
                    m_ee2 = p.op("vector", lambda v, W=W: v.tensor_copy(
                        bass.AP(tee, (32 * W) * 512, [[512, 32], [32, 2], [64, 8], [1, 32]]),
                        bass.AP(prot, (32 * W) * 2048 + (W + 1) % 4,
                                [[2048, 32], [1024, 2], [128, 8], [4, 32]])), inc=sV)
                    p.op("gpsimd", lambda g, W=W: g.affine_select(
                        dd[32 * W:32 * W + 32, :], tdd[32 * W:32 * W + 32, :],
                        pattern=[[0, 16], [1, 32]], compare_op=ALU.is_ge,
                        fill=0.0, channel_multiplier=-1),
                        waits=[(sV, m_ee2), (sT, rotT_T)], inc=sP)
                    p.op("gpsimd", lambda g, W=W: g.affine_select(
                        ee[32 * W:32 * W + 32, :], tee[32 * W:32 * W + 32, :],
                        pattern=[[0, 16], [-1, 32]], compare_op=ALU.is_ge,
                        fill=0.0, base=-1, channel_multiplier=1), inc=sP)
                sel_P = p.val(sP)
                if blkw == 0:
                    p.ckpt(f"sel_{seq}_{tw}")

                bankmark = {}
                for hb in range(4):
                    if hb == 0:
                        bw = [(sP, sel_P), (sV, vt_V), (sS, p.val(sS)), (sV, p.val(sV))]
                    else:
                        bw = [(sS, p.val(sS)), (sV, p.val(sV))]
                    for hh in range(4):
                        h = hb * 4 + hh
                        bA = hh % 2
                        slotA = bA * 512 + (hh // 2) * 65
                        bB = 1 - bA
                        slotB = bB * 512 + (hh // 2) * 65 + 130
                        wt0 = list(bw) if hh == 0 else []
                        for V in range(4):
                            w = list(wt0) if V == 0 else []
                            key = (V, bA)
                            if key in bankmark:
                                w.append((sT, bankmark[key]))
                            bankmark[key] = p.op("tensor", lambda t, h=h, V=V, slotA=slotA, blk=blk: t.matmul(
                                pout[32 * V:32 * V + 32, slotA:slotA + 65],
                                dd[32 * V:32 * V + 32, h * 32:(h + 1) * 32],
                                bass.AP(valtok, (32 * V) * VTS + (blk + 1) * 1040 + h * 65,
                                        [[VTS, 32], [1, 65]]),
                                start=True, stop=True,
                                tile_position=(32 * V, 32 * V)),
                                waits=w, inc=sT)
                        for V in range(4):
                            q = 0 if V == 0 else V - 1
                            outbase = 0 if V == 0 else 32 * V
                            key = (outbase // 32, bB)
                            w = []
                            if key in bankmark:
                                w.append((sT, bankmark[key]))
                            if V == 0:
                                bankmark[key] = p.op("tensor", lambda t, h=h, slotB=slotB, blk=blk: t.matmul(
                                    pout[0:32, slotB:slotB + 65],
                                    ee[96:128, h * 32:(h + 1) * 32],
                                    bass.AP(valtok, 96 * VTS + blk * 1040 + h * 65,
                                            [[VTS, 32], [1, 65]]),
                                    start=True, stop=True,
                                    tile_position=(96, 0)), waits=w, inc=sT)
                            else:
                                bankmark[key] = p.op("tensor", lambda t, h=h, V=V, slotB=slotB, blk=blk: t.matmul(
                                    pout[32 * V:32 * V + 32, slotB:slotB + 65],
                                    ee[32 * (V - 1):32 * V, h * 32:(h + 1) * 32],
                                    bass.AP(valtok, (32 * (V - 1)) * VTS + (blk + 1) * 1040 + h * 65,
                                            [[VTS, 32], [1, 65]]),
                                    start=True, stop=True,
                                    tile_position=(32 * (V - 1), 32 * V)), waits=w, inc=sT)
                    band_T = p.val(sT)
                    if blkw == 0 and hb == 0:
                        p.ckpt(f"band_{seq}_{tw}")
                    for hh in range(4):
                        h = hb * 4 + hh
                        bA = hh % 2
                        slotA = bA * 512 + (hh // 2) * 65
                        slotB = (1 - bA) * 512 + (hh // 2) * 65 + 130
                        m_ca = p.op("scalar", lambda sc, slotA=slotA: sc.copy(
                            sumA[:, :], pout[:, slotA:slotA + 65]),
                            waits=[(sT, band_T)] if hh == 0 else [], inc=sS)
                        m_sm = p.op("vector", lambda v, slotB=slotB: v.scalar_tensor_tensor(
                            sumB[:, :], pout[:, slotB:slotB + 65], 1.0, sumA[:, :],
                            ALU.mult, ALU.add), waits=[(sS, m_ca)], inc=sV)
                        m_r = p.op("vector", lambda v, h=h: v.reciprocal(
                            rz[:, h:h + 1], sumB[:, 64:65]), waits=[(sV, m_sm)], inc=sV)
                        p.op("scalar", lambda sc, h=h, blkw=blkw: sc.activation(
                            attntok[:, blkw * 1024 + h * 64: blkw * 1024 + (h + 1) * 64],
                            sumB[:, 0:64], AF.Identity,
                            scale=rz[:, h:h + 1]), waits=[(sV, m_r)], inc=sS)


def quantize_weights(nc, p, L):
    sD, sG, sT, sV, sS = L["sD"], L["sG"], L["sT"], L["sV"], L["sS"]
    sQ = (L["sQa"], L["sQb"])
    d_w1t, d_w2t = L["d_w1t"], L["d_w2t"]
    w1q, w2q, scol, ucol, vcol, ubias = (
        L["w1q"], L["w2q"], L["scol"], L["ucol"], L["vcol"], L["ubias"])
    b2lnc, b2lnb, b2fc, g2c = L["b2lnc"], L["b2lnb"], L["b2fc"], L["g2c"]
    onescf, onescb, onesrow, identf = L["onescf"], L["onescb"], L["onesrow"], L["identf"]
    qrow = L["qrow"]

    p.op("vector", lambda v: v.tensor_copy(b2lnb[:, :], b2lnc[:, :]), inc=sV)

    from contextlib import ExitStack
    with ExitStack() as _sq:
        qscr = _sq.enter_context(nc.sbuf_tensor("qscr", [128, 2 * 4096], F32))
        qm = _sq.enter_context(nc.sbuf_tensor("qm", [128, 4096], F32))
        qsg = _sq.enter_context(nc.sbuf_tensor("qsg", [128, 4096], F32))
        qpart = _sq.enter_context(nc.sbuf_tensor("qpart", [128, 40], F32))
        pq = _sq.enter_context(nc.psum_tensor("pq", [128, 8], F32))
        pq1 = _sq.enter_context(nc.psum_tensor("pq1", [1, 256], F32))

        for wi, (dtens, nch, chw, qdst, scoli) in enumerate(
                [(d_w1t, 8, 4096, w1q, 0), (d_w2t, 32, 1024, w2q, 2)]):
            red = {}
            for k in range(nch):
                o = (k % 2) * 4096
                wts = [(sV, red[k - 2])] if k >= 2 else [(sV, p.val(sV))]
                m_l = p.op("sync", lambda s, dtens=dtens, k=k, chw=chw, o=o: s.dma_start(
                    qscr[:, o:o + chw],
                    bass.AP(dtens, k * 128 * chw, [[chw, 128], [1, chw]])),
                    waits=wts, inc=sQ[k % 2], n=16)
                red[k] = p.op("vector", lambda v, k=k, chw=chw, o=o: v.tensor_reduce(
                    qpart[:, k:k + 1], qscr[:, o:o + chw], AX.X, ALU.add,
                    apply_absolute_value=True), waits=[(sQ[k % 2], m_l)], inc=sV)
            m_r2 = p.op("vector", lambda v, nch=nch: v.tensor_reduce(
                qpart[:, 36:37], qpart[:, 0:nch], AX.X, ALU.add), inc=sV)
            m_t = p.op("tensor", lambda t: t.matmul(
                pq1[0:1, 0:1], onescf[:, 0:1], qpart[:, 36:37], start=True, stop=True),
                waits=[(sV, m_r2)], inc=sT)

            def f_scal(v, nch=nch, chw=chw):
                v.tensor_scalar(qrow[0:1, 0:1], pq1[0:1, 0:1],
                                1.0 / (nch * 128 * chw), None, ALU.mult)
                v.drain()
                v.tensor_scalar(qrow[0:1, 0:1], qrow[0:1, 0:1], 1e-5, None, ALU.max)
                v.drain()
                return v.reciprocal(qrow[0:1, 1:2], qrow[0:1, 0:1])
            m_v = p.op("vector", f_scal, waits=[(sT, m_t)], inc=sV)
            m_t2 = p.op("tensor", lambda t: t.matmul(
                pq[:, 0:1], onesrow[0:1, :], qrow[0:1, 1:2], start=True, stop=True),
                waits=[(sV, m_v)], inc=sT)
            m_t3 = p.op("tensor", lambda t: t.matmul(
                pq[:, 1:2], onesrow[0:1, :], qrow[0:1, 0:1], start=True, stop=True),
                waits=[(sT, m_t2)], inc=sT)
            scol_V = p.op("vector", lambda v, scoli=scoli: v.tensor_copy(
                scol[:, scoli:scoli + 2], pq[:, 0:2]), waits=[(sT, m_t3)], inc=sV)

            lastq = {}
            for k in range(nch):
                o = (k % 2) * 4096
                wts = [(sV, lastq[k - 2])] if k >= 2 else [(sV, p.val(sV))]
                m_l = p.op("sync", lambda s, dtens=dtens, k=k, chw=chw, o=o: s.dma_start(
                    qscr[:, o:o + chw],
                    bass.AP(dtens, k * 128 * chw, [[chw, 128], [1, chw]])),
                    waits=wts, inc=sQ[k % 2], n=16)

                def f_q(v, chw=chw, scoli=scoli, o=o):
                    v.tensor_scalar(qscr[:, o:o + chw], qscr[:, o:o + chw],
                                    scol[:, scoli:scoli + 1], None, ALU.mult)
                    v.drain()
                    v.scalar_tensor_tensor(qm[:, 0:chw], qscr[:, o:o + chw], -1.0,
                                           qscr[:, o:o + chw], ALU.mult, ALU.max)
                    v.drain()
                    return v.tensor_scalar(qm[:, 0:chw], qm[:, 0:chw], 0.5, None, ALU.is_ge)
                m_q = p.op("vector", f_q, waits=[(sQ[k % 2], m_l), (sV, scol_V)], inc=sV)
                m_sg2 = p.op("scalar", lambda sc, chw=chw, o=o: sc.activation(
                    qsg[:, 0:chw], qscr[:, o:o + chw], AF.Sign), waits=[(sV, m_q)], inc=sS)
                lastq[k] = p.op("vector", lambda v, qdst=qdst, k=k, chw=chw: v.tensor_tensor(
                    qdst[:, k * chw:(k + 1) * chw], qm[:, 0:chw], qsg[:, 0:chw], ALU.mult),
                    waits=[(sS, m_sg2)], inc=sV)
        q_done_V = p.val(sV)

        for m in range(8):
            def f_uc(t, m=m):
                for k in range(32):
                    inst = t.matmul(pq[:, 2:3],
                                    w2q[:, k * 1024 + m * 128: k * 1024 + (m + 1) * 128],
                                    b2lnb[:, k:k + 1], start=(k == 0), stop=(k == 31))
                return inst
            w = [(sV, q_done_V)] if m == 0 else [(sV, p.val(sV))]
            m_t = p.op("tensor", f_uc, waits=w, inc=sT)
            p.op("vector", lambda v, m=m: v.tensor_copy(ucol[:, m:m + 1], pq[:, 2:3]),
                 waits=[(sT, m_t)], inc=sV)
        uc_V = p.val(sV)

        for k in range(32):
            p.op("vector", lambda v, k=k: v.tensor_scalar(
                w2q[:, k * 1024:(k + 1) * 1024], w2q[:, k * 1024:(k + 1) * 1024],
                g2c[:, k:k + 1], None, ALU.mult),
                waits=[(sV, uc_V), (sT, p.val(sT))] if k == 0 else [], inc=sV)
        fold_V = p.val(sV)

        for m in range(8):
            def f_vc(t, m=m):
                for k in range(32):
                    inst = t.matmul(pq[:, 3:4],
                                    w2q[:, k * 1024 + m * 128: k * 1024 + (m + 1) * 128],
                                    onescb[:, 0:1], start=(k == 0), stop=(k == 31))
                return inst
            w = [(sV, fold_V)] if m == 0 else [(sV, p.val(sV))]
            m_t = p.op("tensor", f_vc, waits=w, inc=sT)
            p.op("vector", lambda v, m=m: v.tensor_scalar(
                vcol[:, m:m + 1], pq[:, 3:4], scol[:, 3:4], None, ALU.mult),
                waits=[(sT, m_t)], inc=sV)

        def f_ub(v):
            v.tensor_scalar(ubias[:, 0:8], ucol[:, 0:8], scol[:, 3:4], None, ALU.mult)
            v.drain()
            return v.tensor_tensor(ubias[:, 0:8], ubias[:, 0:8], b2fc[:, 0:8], ALU.add)
        p.op("vector", f_ub, waits=[(sV, p.val(sV))], inc=sV)


def phase2_front(nc, p, tile, L):
    sD, sG, sT, sV, sS = L["sD"], L["sG"], L["sT"], L["sV"], L["sS"]
    sX = L["sX"]
    dx, d_cp = L["dx"], L["d_cp"]
    ln1gb, ln1bb = L["ln1gb"], L["ln1bb"]
    xtok2, xn2, ln1tok, ln1fm = L["xtok2"], L["xn2"], L["ln1tok"], L["ln1fm"]
    cptw = L["cptw"]
    p2m = L.setdefault("p2m", {})

    seq, twl = tile // 8, tile % 8
    tok0 = twl * TW2
    par3 = tile % 3
    xb = par3 * 2048
    cb = (tile % 2) * 2048

    wx = [(sV, p2m[("x", par3)])] if ("x", par3) in p2m else []
    m_lx = p.op("sync", lambda s: s.dma_start(
        bass.AP(xtok2, xb, [[3 * 2048, 128], [1024, 2], [1, 1024]]),
        bass.AP(dx, (seq * T + tok0) * C, [[C, 128], [128 * C, 2], [1, C]])),
        waits=wx, inc=sX, n=16)
    wc = [(sV, p2m["outV"])] if "outV" in p2m else []
    m_lc = p.op("gpsimd", lambda g: g.dma_start(
        bass.AP(cptw, cb, [[2 * 8 * 256, 128], [256, 8], [1, 256]]),
        bass.AP(d_cp, (seq * 8) * 128 * T + tok0, [[T, 128], [128 * T, 8], [1, 256]])),
        waits=wc, inc=sG, n=16)

    lb = (tile % 2) * 2048
    w1pred = p2m.get("sTend", 0)
    for tb in range(2):
        o = tb * 1024
        m_xn = ln_stats_apply(nc, p, L, xtok2, (xb + o, xb + o + 1024),
                              xn2, (0, 1024), [(sX, m_lx)])
        m_l1 = ln_stats_apply(nc, p, L, xn2, (0, 1024), ln1tok, (0, 1024),
                              [(sV, m_xn), (sD, 1)])
        for cc in range(8):
            w = []
            if cc == 0:
                w = [(sV, m_l1)]
                if w1pred:
                    w.append((sT, w1pred))
            p.op("sync", lambda s, cc=cc, tb=tb: s.dma_start(
                bass.AP(ln1fm, lb + cc * 256 + tb * 128, [[2 * 8 * 256, 128], [1, 128]]),
                bass.AP(ln1tok, cc * 128, [[1024, 128], [1, 128]]),
                transpose=True), waits=w, inc=sD, n=16)
    lnfm_D = p.val(sD)

    for cc in range(8):
        p.op("vector", lambda v, cc=cc: v.tensor_scalar(
            ln1fm[:, lb + cc * 256:lb + (cc + 1) * 256],
            ln1fm[:, lb + cc * 256:lb + (cc + 1) * 256],
            ln1gb[:, cc:cc + 1], ln1bb[:, cc:cc + 1], ALU.mult, ALU.add),
            waits=[(sD, lnfm_D)] if cc == 0 else [], inc=sV)
    p2m[("ln1", tile)] = p.val(sV)
    p2m[("lc", par3)] = m_lc


def phase2_back(nc, p, tile, L):
    sD, sG, sT, sV, sS, sO = L["sD"], L["sG"], L["sT"], L["sV"], L["sS"], L["sO"]
    dout = L["dout"]
    identf = L["identf"]
    w1q, w2q, scol, ubias, vcol = L["w1q"], L["w2q"], L["scol"], L["ubias"], L["vcol"]
    b1c = L["b1c"]
    onescb, onesrow = L["onescb"], L["onesrow"]
    xtok2, ln1fm = L["xtok2"], L["ln1fm"]
    h1g, sqt, rrow = L["h1g"], L["sqt"], L["rrow"]
    rrep, murep = L["rrep"], L["murep"]
    cptw, sumt, outtok = L["cptw"], L["sumt"], L["outtok"]
    p2m = L["p2m"]

    seq, twl = tile // 8, tile % 8
    tok0 = twl * TW2
    par3 = tile % 3
    xb = par3 * 2048
    cb = (tile % 2) * 2048
    lb = (tile % 2) * 2048
    ln1_V = p2m[("ln1", tile)]
    m_lc = p2m[("lc", par3)]

    with nc.psum_tensor(f"ph1a_{tile}", [128, 256], F32) as ph1a, \
         nc.psum_tensor(f"ph1b_{tile}", [128, 256], F32) as ph1b, \
         nc.psum_tensor(f"pz1_{tile}", [1, 256], F32) as pz1, \
         nc.psum_tensor(f"pz2_{tile}", [1, 256], F32) as pz2, \
         nc.psum_tensor(f"pR_{tile}", [128, 256], F32) as pR, \
         nc.psum_tensor(f"pMu_{tile}", [128, 256], F32) as pMu, \
         nc.psum_tensor(f"po_{tile}", [128, 256], F32) as po:
        ph1s = (ph1a, ph1b)
        pft = ph1a
        gel = {}
        sqm = {}

        def emit_z(mm):
            zw = [(sS, gel[mm])]
            if mm == 0 and "statsV" in p2m:
                zw.append((sV, p2m["statsV"]))
            p.op("tensor", lambda t, mm=mm: t.matmul(
                pz1[0:1, 0:256], onescb[:, 0:1], h1g[:, mm * 256:(mm + 1) * 256],
                start=(mm == 0), stop=(mm == 31)), waits=zw, inc=sT)
            sqw2 = [(sS, sqm[mm])]
            if mm == 0 and "statsV" in p2m:
                sqw2.append((sV, p2m["statsV"]))
            return p.op("tensor", lambda t, mm=mm: t.matmul(
                pz2[0:1, 0:256], onescb[:, 0:1], sqt[:, :],
                start=(mm == 0), stop=(mm == 31)), waits=sqw2, inc=sT)

        sq_mm = 0
        for m in range(32):
            def f_w1(t, m=m):
                ph1 = ph1s[m % 2]
                for k in range(8):
                    inst = t.matmul(ph1[:, :],
                                    w1q[:, k * 4096 + m * 128: k * 4096 + (m + 1) * 128],
                                    ln1fm[:, lb + k * 256:lb + (k + 1) * 256],
                                    start=(k == 0), stop=(k == 7))
                return inst
            w = []
            if m == 0:
                w = [(sV, max(p2m.get("outV", 0), ln1_V))]
            if m < 2:
                if ("ph1S", m % 2) in p2m:
                    w.append((sS, p2m[("ph1S", m % 2)]))
            else:
                w = [(sS, gel[m - 2])]
            m_t = p.op("tensor", f_w1, waits=w, inc=sT)
            gel[m] = p.op("scalar", lambda sc, m=m: sc.activation(
                h1g[:, m * 256:(m + 1) * 256], ph1s[m % 2][:, :], AF.Gelu,
                bias=b1c[:, m:m + 1], scale=scol[:, 1:2]), waits=[(sT, m_t)], inc=sS)
            if m >= 1:
                sq_mm = emit_z(m - 1)
            sqw = [(sS, gel[m])]
            if sq_mm:
                sqw.append((sT, sq_mm))
            sqm[m] = p.op("scalar", lambda sc, m=m: sc.activation(
                sqt[:, :], h1g[:, m * 256:(m + 1) * 256], AF.Square), waits=sqw, inc=sS)
        emit_z(31)
        z_T = p.val(sT)
        prev_g = gel[31]
        p2m[("ph1S", 0)] = gel[30]
        p2m[("ph1S", 1)] = gel[31]

        def f_stats(v):
            v.tensor_scalar(rrow[0:1, 256:512], pz1[0:1, 0:256], 1.0 / HID, None, ALU.mult)
            v.tensor_scalar(rrow[0:1, 0:256], pz2[0:1, 0:256], 1.0 / HID, None, ALU.mult)
            v.drain()
            v.tensor_tensor(rrow[0:1, 512:768], rrow[0:1, 256:512],
                            rrow[0:1, 256:512], ALU.mult)
            v.drain()
            return v.tensor_tensor(rrow[0:1, 512:768], rrow[0:1, 0:256],
                                   rrow[0:1, 512:768], ALU.subtract)
        m_v = p.op("vector", f_stats, waits=[(sT, z_T)], inc=sV)
        p2m["statsV"] = m_v
        m_s = p.op("scalar", lambda sc: sc.activation(
            rrow[0:1, 512:768], rrow[0:1, 512:768], AF.Sqrt, bias=float(EPS)),
            waits=[(sV, m_v)], inc=sS)

        def f_r(v):
            v.reciprocal(rrow[0:1, 768:1024], rrow[0:1, 512:768])
            v.drain()
            v.tensor_tensor(rrow[0:1, 0:256], rrow[0:1, 256:512],
                            rrow[0:1, 768:1024], ALU.mult)
            v.drain()
            v.tensor_scalar(rrow[0:1, 0:256], rrow[0:1, 0:256],
                            -1.0, None, ALU.mult)
            return v.tensor_scalar(rrow[0:1, 768:1024], rrow[0:1, 768:1024],
                                   scol[0:1, 3:4], None, ALU.mult)
        m_r = p.op("vector", f_r, waits=[(sS, m_s)], inc=sV)

        m_tR = p.op("tensor", lambda t: t.matmul(
            pR[:, :], onesrow[0:1, :], rrow[0:1, 768:1024], start=True, stop=True),
            waits=[(sV, m_r)], inc=sT)
        m_tM = p.op("tensor", lambda t: t.matmul(
            pMu[:, :], onesrow[0:1, :], rrow[0:1, 0:256], start=True, stop=True),
            inc=sT)
        m_vR = p.op("vector", lambda v: v.tensor_copy(rrep[:, :], pR[:, :]),
                    waits=[(sT, m_tM)], inc=sV)
        p.op("vector", lambda v: v.tensor_copy(murep[:, :], pMu[:, :]), inc=sV)

        prev_v1 = 0
        for m in range(8):
            def f_w2(t, m=m):
                for k in range(32):
                    inst = t.matmul(po[:, :],
                                    w2q[:, k * 1024 + m * 128: k * 1024 + (m + 1) * 128],
                                    h1g[:, k * 256:(k + 1) * 256],
                                    start=(k == 0), stop=(k == 31))
                return inst
            if m == 0:
                w = [(sS, prev_g)]
                if "poV" in p2m:
                    w.append((sV, p2m["poV"]))
            else:
                w = [(sV, prev_v1)]
            m_w2 = p.op("tensor", f_w2, waits=w, inc=sT)

            def f_sum(v, m=m):
                v.tensor_tensor(sumt[:, :], po[:, :], rrep[:, :], ALU.mult)
                v.drain()
                v.scalar_tensor_tensor(sumt[:, :], murep[:, :], vcol[:, m:m + 1],
                                       sumt[:, :], ALU.mult, ALU.add)
                v.drain()
                return v.scalar_tensor_tensor(sumt[:, :],
                                              cptw[:, cb + m * 256:cb + (m + 1) * 256],
                                              ubias[:, m:m + 1], sumt[:, :],
                                              ALU.add, ALU.add)
            prev_v1 = p.op("vector", f_sum, waits=[(sT, m_w2), (sG, m_lc)], inc=sV)
            m_st = prev_v1
            for tb in range(2):
                prev_tt = p.op("tensor", lambda t, tb=tb: t.transpose(
                    pft[:, 0:128], sumt[:, tb * 128:(tb + 1) * 128], identf[:, :]),
                    waits=[(sV, m_st)], inc=sT)
                ow = [(sT, prev_tt)]
                if m == 0 and tb == 0 and tile >= 1:
                    ow.append((sO, 1))
                m_st = p.op("vector", lambda v, tb=tb, m=m: v.scalar_tensor_tensor(
                    outtok[:, tb * 1024 + m * 128: tb * 1024 + (m + 1) * 128],
                    pft[:, 0:128], 1.0,
                    xtok2[:, xb + tb * 1024 + m * 128: xb + tb * 1024 + (m + 1) * 128],
                    ALU.mult, ALU.add), waits=ow, inc=sV)
        out_V = p.val(sV)
        p2m[("x", par3)] = out_V
        p2m["outV"] = out_V
        p2m["poV"] = prev_v1
        p2m["sTend"] = p.val(sT)

        p.op("gpsimd", lambda g: g.dma_start(
            bass.AP(dout, (seq * T + tok0) * C, [[C, 128], [128 * C, 2], [1, C]]),
            bass.AP(outtok, 0, [[2 * 1024, 128], [1024, 2], [1, 1024]])),
            waits=[(sV, out_V)], inc=sO, n=16)


def _host_layouts(inputs):
    f = np.float32
    W_sf = np.ascontiguousarray(np.asarray(inputs["L0_attn_W_sf"], f))
    w_bond = np.ascontiguousarray(np.asarray(inputs["L0_attn_w_bond"], f))
    w_dmg = np.ascontiguousarray(np.asarray(inputs["L0_attn_w_dmg"], f))
    sdw = np.zeros((2, 128, 128), f)
    for half in range(2):
        for hh in range(4):
            sdw[half, (half * 4 + hh) * 16:(half * 4 + hh) * 16 + 16,
                hh * 32:(hh + 1) * 32] = W_sf
    sdot = np.zeros((16, 128, 128), f)
    for jj in range(16):
        for hh in range(4):
            sdot[jj, hh * 32:hh * 32 + 16, hh * 16 + jj] = w_bond
            sdot[jj, hh * 32 + 16:hh * 32 + 32, 64 + hh * 16 + jj] = w_dmg
    wposx = np.zeros((16, 128), f)
    W_pos = np.asarray(inputs["L0_attn_W_pos"], f)
    for hh in range(4):
        wposx[:, hh * 32:hh * 32 + 16] = W_pos
    cvecm = np.zeros((128, 8), f)
    b_sf = np.asarray(inputs["L0_attn_b_sf"], f)
    for hh in range(4):
        cvecm[hh * 32:(hh + 1) * 32, 0] = b_sf
    cvecm[:, 1] = float(np.asarray(inputs["L0_attn_b_dmg"], f))
    relt = np.ascontiguousarray(np.asarray(inputs["L0_attn_rel_pos_emb"], f).T)

    def colchunks(vec, n):
        return np.ascontiguousarray(np.asarray(vec, f).reshape(n, 128).T)

    return dict(
        wdisp=np.ascontiguousarray(np.asarray(inputs["L0_attn_W_disp"], f)),
        wval=np.ascontiguousarray(np.asarray(inputs["L0_attn_W_val"], f)),
        wcproj=np.ascontiguousarray(np.asarray(inputs["L0_attn_W_cproj"], f)),
        w1t=np.ascontiguousarray(np.asarray(inputs["L0_ffn_w1"], f).T),
        w2t=np.ascontiguousarray(np.asarray(inputs["L0_ffn_w2"], f).T),
        sdw=sdw, sdot=sdot,
        ident=np.eye(128, dtype=f),
        cvec=cvecm, wposx=wposx, relt=relt,
        ln1g=colchunks(inputs["L0_ffn_ln1_g"], 8),
        ln1b=colchunks(inputs["L0_ffn_ln1_b"], 8),
        b1c=colchunks(inputs["L0_ffn_b1"], 32),
        g2c=colchunks(inputs["L0_ffn_ln2_g"], 32),
        b2lnc=colchunks(inputs["L0_ffn_ln2_b"], 32),
        b2fc=colchunks(inputs["L0_ffn_b2"], 8),
        onesrow=np.ones((1, 128), f),
        mask0=_mask0(),
        skm=_skm(),
    )


def _skm():
    m = np.zeros((128, 32), np.float32)
    for q in range(128):
        m[q, q // 4] = 1.0
    return m


def _mask0():
    m = np.ones((128, 1024), np.float32)
    for q in range(128):
        t = (q % 4) * 32 + q // 4
        if t < 31:
            for H in range(16):
                m[q, H * 64:H * 64 + (31 - t)] = 0.0
                m[q, H * 64 + 32:H * 64 + 32 + (31 - t)] = 0.0
    return m


def kernel(**inputs):
    nc = build_program()
    lay = _host_layouts(inputs)
    x = np.ascontiguousarray(np.asarray(inputs["L0_attn_x"], np.float32))
    in_maps = []
    for c in range(NCORES):
        m = dict(lay)
        m["x"] = np.ascontiguousarray(x[c * SPC:(c + 1) * SPC])
        in_maps.append(m)
    res = run_bass_kernel_spmd(nc, in_maps, core_ids=list(range(NCORES)))
    out = np.concatenate([r["out"] for r in res.results], axis=0)
    return out.astype(np.float32)
